# version 30
# baseline (speedup 1.0000x reference)
import numpy as np
from contextlib import ExitStack

import concourse.bass as bass
import concourse.mybir as mybir
from concourse.bass_utils import run_bass_kernel_spmd

F32 = mybir.dt.float32
BF16 = mybir.dt.bfloat16
I32 = mybir.dt.int32
AF = mybir.ActivationFunctionType
ALU = mybir.AluOpType

D = 1024
T = 2048
NCH = 8
TAU = 512
NTAU = T // TAU
NSUB = TAU // 128
S2 = 2048
NS = 16
SIG = 512
NSIG = S2 // SIG
NKS = SIG // 128
GRP = 2
CC = 8.0
EPS = 1e-5
HALF = T // 2
NH = HALF // TAU
DEBUG = False


class Sem:
    def __init__(self, es, nc, name, owner=None):
        self.h = es.enter_context(nc.semaphore(name))
        self.n = 0
        self.owner = owner
        self.bar_n = 0

    def inc(self, instr, k=1):
        instr.then_inc(self.h, k)
        self.n += k
        return self.n

    def dinc(self, instr):
        return self.inc(instr, 16)


class Prog:
    def __init__(self, nc):
        self.nc = nc
        self.es = ExitStack()
        es = self.es
        self.sb = lambda *a: es.enter_context(nc.sbuf_tensor(*a))

        dt = lambda n, s, d, k: nc.dram_tensor(n, s, d, kind=k).ap()
        self.x_in = dt("L0_attn_a_x", [T, D], F32, "ExternalInput")
        for w in ["Wp1", "Wp2", "conv_w", "Wa", "Wx", "Wp3"]:
            setattr(self, w, dt("L0_attn_a_" + w, [D, D], F32, "ExternalInput"))
        for v in ["bp1", "bp2", "conv_b", "ba", "bx", "Lam", "bp3"]:
            setattr(self, v, dt("L0_attn_a_" + v, [D], F32, "ExternalInput"))
        for v in ["x_re", "x_im", "nu_log", "theta_log", "gamma_log"]:
            setattr(self, v, dt("L0_attn_b_" + v, [S2], F32, "ExternalInput"))
        self.B_re = dt("L0_attn_b_B_re", [S2, D], F32, "ExternalInput")
        self.B_im = dt("L0_attn_b_B_im", [S2, D], F32, "ExternalInput")
        self.C_re = dt("L0_attn_b_C_re", [D, S2], F32, "ExternalInput")
        self.C_im = dt("L0_attn_b_C_im", [D, S2], F32, "ExternalInput")
        self.Dm = dt("L0_attn_b_D", [D, D], F32, "ExternalInput")
        self.qw = dt("L0_ffn_qweights", [6, D, D], I32, "ExternalInput")
        self.qs = dt("L0_ffn_scales", [6, D, D // 16], F32, "ExternalInput")
        self.qb = dt("L0_ffn_biases", [6, D], F32, "ExternalInput")
        self.la = dt("L0_ffn_lora_a", [6, 32, D], F32, "ExternalInput")
        self.lb = dt("L0_ffn_lora_b", [6, D, 32], F32, "ExternalInput")
        self.lng = dt("L0_ffn_ln_gamma", [3, D], F32, "ExternalInput")
        self.lnb = dt("L0_ffn_ln_beta", [3, D], F32, "ExternalInput")
        self.y_out = dt("y", [T, D], F32, "ExternalOutput")

        self.scr = {}
        for w, shp in [("Wp1", [D, D]), ("Wp2", [D, D]), ("conv_w", [D, D]),
                       ("Wa", [D, D]), ("Wx", [D, D]), ("Wp3", [D, D]),
                       ("B_re", [S2, D]), ("B_im", [S2, D]),
                       ("C_re", [D, S2]), ("C_im", [D, S2]), ("Dm", [D, D])]:
            self.scr[w] = dt("scr_" + w, shp, BF16, "Internal")
        self.scr_weff = dt("scr_weff", [6, D, D], BF16, "Internal")
        self.scr_lb = dt("scr_lb", [6, D, 32], BF16, "Internal")
        self.scr_r = [dt("scr_r%d" % i, [NTAU, 128, NCH, TAU], F32, "Internal")
                      for i in range(4)]
        if DEBUG:
            self.dbg_y1 = dt("dbg_y1", [128, NCH, TAU], BF16, "Internal")
            self.dbg_y2 = dt("dbg_y2", [128, NCH, TAU], BF16, "Internal")
            self.dbg_h = dt("dbg_h", [128, NCH, HALF], F32, "Internal")
            self.dbg_hb = dt("dbg_hb", [128, NCH, HALF], BF16, "Internal")
            self.dbg_st = dt("dbg_st", [1, 2, TAU], F32, "Internal")

        sb = self.sb
        self.ident = sb("ident", [128, 128], F32)
        self.identb = sb("identb", [128, 128], BF16)
        self.ones = sb("ones", [128, 1], BF16)
        self.onesr = sb("onesr", [1, 128], F32)
        self.cv = sb("cv", [128, 352], F32)
        self.cv_off = {}
        off = 0
        for name, w in [("bp1", 8), ("bp2", 8), ("conv_b", 8), ("ba", 8),
                        ("bx", 8), ("bp3", 8), ("Lam", 8), ("splam", 8), ("nsplam", 8),
                        ("g2", 16), ("g2n", 16), ("cre2", 16), ("cim2n", 16),
                        ("x_re", 16), ("x_im", 16), ("nu_log", 16),
                        ("theta_log", 16), ("gamma_log", 16),
                        ("qb", 48), ("lng", 24), ("lnb", 24), ("tmp", 32)]:
            self.cv_off[name] = off
            off += w
        assert off <= 352
        self.stats = sb("stats", [1, 2, T], F32)
        self.stats_i = sb("stats_i", [1, 2, TAU], F32)
        self.fscr = self.stats_i[0:1, 0, :]
        self.XN = sb("XN", [128, NCH, T], BF16)
        self.ps = es.enter_context(nc.psum_tensor("ps", [128, 4096], F32))

        self.dma_sems = []

        def S(n, owner=None):
            sm = Sem(es, nc, n, owner)
            if owner is not None:
                self.dma_sems.append(sm)
            return sm
        self.g16 = S("g16", "GE")
        self.cast16 = S("cast16", "GE")
        self.cstB = S("castB", "GE")
        self.vld = S("vld", "GE")
        self.o16 = S("o16", "SY")
        self.o16b = S("o16b", "SY")
        self.cstA2 = S("castA2", "GE")
        self.wsld = [S("wsld%d" % i, "SY") for i in range(6)]
        self.ste = S("ste")
        self.sse = S("sse")
        self.sve = S("sve")
        self.sgp = S("sgp")
        self.pb = S("pb")
        self.S = S

        self.TE, self.SE, self.VE = nc.tensor, nc.scalar, nc.vector
        self.GE, self.SY = nc.gpsimd, nc.sync
        self.last = {}
        self.pend = {}
        self.statfin_ve = 0
        self.bcast_ve = 0
        self.bank_mark = {}
        self.ctx = "init"
        self.te_log = []

    def _eng(self, e):
        return {id(self.TE): "TE", id(self.SE): "SE", id(self.VE): "VE",
                id(self.GE): "GE", id(self.SY): "SY"}[id(e)]

    def em(self, e, meth, *a, **kw):
        name = self._eng(e)
        pend = self.pend.pop(name, None)
        if pend is not None:
            for sem, tgt in pend:
                if tgt > 0:
                    e.wait_ge(sem.h, tgt)
        r = getattr(e, meth)(*a, **kw)
        if name == "TE" and meth == "matmul":
            self.te_log.append((r.ins.emission_index, self.ctx))
        self.last[name] = (r, meth.startswith("dma_start"))
        return r

    def wait(self, e, sem, tgt=None):
        name = self._eng(e)
        pend = self.pend.pop(name, None)
        if pend is not None:
            for s2, t2 in pend:
                if t2 > 0:
                    e.wait_ge(s2.h, t2)
        t = tgt if tgt is not None else sem.n
        if t > 0:
            e.wait_ge(sem.h, t)

    def barrier(self, dma_sems=()):
        engs = {"TE": self.TE, "SE": self.SE, "VE": self.VE,
                "GE": self.GE, "SY": self.SY}
        for qname in ("GE", "SY"):
            owed = [sm for sm in self.dma_sems
                    if sm.owner == qname and sm.n > sm.bar_n]
            if owed or qname in self.last:
                e = engs[qname]
                for sm in owed:
                    e.wait_ge(sm.h, sm.n)
                    sm.bar_n = sm.n
                d = e.drain()
                self.pb.inc(d, 1)
                self.last.pop(qname, None)
        for name in list(self.last.keys()):
            d = getattr(engs[name], "drain")()
            self.pb.inc(d, 1)
        self.last = {}
        waits = [(self.pb, self.pb.n)]
        for name in ["TE", "SE", "VE", "GE", "SY"]:
            self.pend[name] = list(waits)

    def wait_bank(self, e, b):
        bm = self.bank_mark.get(b)
        if bm is not None:
            self.wait(e, bm[0], bm[1])

    def bank(self, b, n=512, off=0):
        return self.ps[:, 512 * b + off: 512 * b + off + n]

    def bank1(self, b, n=512):
        return self.ps[0:1, 512 * b: 512 * b + n]

    def cvc(self, name, c):
        o = self.cv_off[name] + c
        return self.cv[:, o: o + 1]

    def cvs(self, name, w):
        o = self.cv_off[name]
        return self.cv[:, o: o + w]

    def setup(self):
        GE, VE, SE = self.GE, self.VE, self.SE
        em, g16 = self.em, self.g16
        self.io_scope = ExitStack()
        self.io1 = self.io_scope.enter_context(
            self.nc.sbuf_tensor("io1", [128, 128], F32))
        self.io2 = self.io_scope.enter_context(
            self.nc.sbuf_tensor("io2", [128, 128], F32))
        em(GE, "iota", self.io1[:, :], [[0, 128]], channel_multiplier=1,
           allow_small_or_imprecise_dtypes=True)
        em(GE, "iota", self.io2[:, :], [[1, 128]], channel_multiplier=0,
           allow_small_or_imprecise_dtypes=True)
        em(GE, "memset", self.ones[:, :], 1.0 / D)
        em(GE, "memset", self.onesr[:, :], 1.0)
        em(GE, "drain")
        self.sgp.inc(em(GE, "nop"))

        def ldvec(dst_name, src, w):
            d = em(GE, "dma_start", self.cvs(dst_name, w),
                   src.rearrange("(c p) -> p c", p=128),
                   allow_slow_non_contiguous=True)
            self.vld.dinc(d)
        for n, src in [("bp1", self.bp1), ("bp2", self.bp2),
                       ("conv_b", self.conv_b), ("ba", self.ba),
                       ("bx", self.bx), ("bp3", self.bp3), ("Lam", self.Lam)]:
            ldvec(n, src, 8)
        for n, src in [("x_re", self.x_re), ("x_im", self.x_im),
                       ("nu_log", self.nu_log), ("theta_log", self.theta_log),
                       ("gamma_log", self.gamma_log)]:
            ldvec(n, src, 16)
        self.vld.dinc(em(GE, "dma_start", self.cvs("qb", 48),
                    self.qb.rearrange("k (c p) -> p (k c)", p=128),
                    allow_slow_non_contiguous=True))
        self.vld.dinc(em(GE, "dma_start", self.cvs("lng", 24),
                    self.lng.rearrange("k (c p) -> p (k c)", p=128),
                    allow_slow_non_contiguous=True))
        self.vld.dinc(em(GE, "dma_start", self.cvs("lnb", 24),
                    self.lnb.rearrange("k (c p) -> p (k c)", p=128),
                    allow_slow_non_contiguous=True))
        vec_done = self.vld.n

        self.wait(VE, self.sgp)
        em(VE, "tensor_tensor", self.ident[:, :], self.io1[:, :],
           self.io2[:, :], ALU.is_equal)
        self.ident_ve = self.sve.inc(em(VE, "drain"))
        self.wait(SE, self.sve)
        self.sse.inc(em(SE, "activation", self.identb[:, :], self.ident[:, :],
                        AF.Copy))

        cvs = self.cvs
        self.wait(SE, self.vld, vec_done)
        tmp = cvs("tmp", 16)
        t8 = tmp[:, 0:8]
        em(SE, "activation", t8, cvs("Lam", 8), AF.Exp, scale=-1.0)
        self.sse.inc(em(SE, "drain"))
        self.wait(VE, self.sse)
        em(VE, "tensor_scalar", t8, t8, 1.0, None, ALU.add)
        self.sve.inc(em(VE, "drain"))
        self.wait(SE, self.sve)
        em(SE, "activation", cvs("splam", 8), t8, AF.Ln)
        em(SE, "drain")
        em(SE, "mul", cvs("splam", 8), cvs("splam", 8), CC)
        em(SE, "drain")
        em(SE, "mul", cvs("nsplam", 8), cvs("splam", 8), -1.0)
        tmp32 = cvs("tmp", 32)
        t16 = tmp32[:, 0:16]
        t16b = tmp32[:, 16:32]
        em(SE, "drain")
        em(SE, "activation", t16, cvs("nu_log", 16), AF.Exp)
        em(SE, "drain")
        em(SE, "activation", cvs("nu_log", 16), t16, AF.Exp, scale=-1.0)
        em(SE, "drain")
        em(SE, "activation", t16b, cvs("theta_log", 16), AF.Exp)
        self.sse.inc(em(SE, "drain"))
        self.wait(VE, self.sse)
        em(VE, "tensor_scalar", t16, t16b, -np.pi / 2, None, ALU.add)
        self.sve.inc(em(VE, "drain"))
        self.wait(SE, self.sve)
        em(SE, "activation", t16, t16, AF.Sin)
        self.sse.inc(em(SE, "drain"))
        lam_mod = cvs("nu_log", 16)
        lre = cvs("g2", 16)
        self.wait(VE, self.sse)
        em(VE, "tensor_tensor", lre, lam_mod, t16, ALU.mult)
        em(VE, "drain")
        em(VE, "tensor_scalar", lre, lre, -1.0, None, ALU.mult)
        self.sve.inc(em(VE, "drain"))
        self.wait(SE, self.sve)
        em(SE, "activation", t16, t16b, AF.Sin)
        self.sse.inc(em(SE, "drain"))
        self.wait(VE, self.sse)
        lim = cvs("theta_log", 16)
        em(VE, "tensor_tensor", lim, lam_mod, t16, ALU.mult)
        em(VE, "drain")
        cre2, cim2n = cvs("cre2", 16), cvs("cim2n", 16)
        em(VE, "tensor_tensor", cre2, lre, cvs("x_re", 16), ALU.mult)
        em(VE, "tensor_tensor", t16, lim, cvs("x_im", 16), ALU.mult)
        em(VE, "drain")
        em(VE, "scalar_tensor_tensor", cre2, t16, -1.0, cre2, ALU.mult, ALU.add)
        em(VE, "drain")
        em(VE, "tensor_scalar", cre2, cre2, 2.0, None, ALU.mult)
        em(VE, "tensor_tensor", cim2n, lre, cvs("x_im", 16), ALU.mult)
        em(VE, "tensor_tensor", t16, lim, cvs("x_re", 16), ALU.mult)
        em(VE, "drain")
        em(VE, "tensor_tensor", cim2n, cim2n, t16, ALU.add)
        em(VE, "drain")
        em(VE, "tensor_scalar", cim2n, cim2n, -2.0, None, ALU.mult)
        self.sve.inc(em(VE, "drain"))
        self.wait(SE, self.sve)
        em(SE, "activation", cvs("g2", 16), cvs("gamma_log", 16), AF.Exp)
        em(SE, "drain")
        em(SE, "mul", cvs("g2", 16), cvs("g2", 16), 2.0)
        em(SE, "drain")
        em(SE, "mul", cvs("g2n", 16), cvs("g2", 16), -1.0)
        em(SE, "drain")

        for w in ["Wp2", "Wp1", "conv_w"]:
            src = getattr(self, w)
            self.cast16.dinc(em(GE, "dma_start", self.scr[w][:, :], src[:, :]))
        self.cast16.dinc(em(GE, "dma_start", self.scr_lb[:, :, :],
                            self.lb[:, :, :]))
        self.cast_done = self.cast16.n
        self.io_scope.close()

    def emit_castA2(self):
        em, GE = self.em, self.GE
        for w in ["Wa", "Wx", "Wp3"]:
            src = getattr(self, w)
            self.cstA2.dinc(em(GE, "dma_start", self.scr[w][:, :], src[:, :]))
        self.castA2_done = self.cstA2.n

    def emit_castB(self):
        em, GE = self.em, self.GE
        for w in ["B_re", "B_im", "C_re", "C_im", "Dm"]:
            src = getattr(self, w)
            self.cstB.dinc(em(GE, "dma_start", self.scr[w][:, :], src[:, :]))
        self.castB_done = self.cstB.n

    def prep_alloc(self, scope):
        nc, em = self.nc, self.em
        x2c = scope
        xi = x2c.bitcast(I32)
        xf = x2c.bitcast(F32)
        P = {}
        P["qi"] = [xi[:, 0:2, :], xi[:, 6:8, :]]
        P["waff"] = [xf[:, 0:2, :], xf[:, 6:8, :]]
        P["sc"] = [xf[:, 3, 0:128].rearrange("p (a b) -> p a b", a=2),
                   xf[:, 3, 128:256].rearrange("p (a b) -> p a b", a=2)]
        P["lab"] = x2c[0:32, 4, 0:1024]
        P["lbT"] = x2c[0:32, 5, 0:1024]
        P["lbb"] = x2c[:, 3, 1024:1280].rearrange("p (c r) -> p c r", c=NCH)
        P["qsem2"] = [self.S("p_q0", "GE"), self.S("p_q1", "GE")]
        P["qmk"] = {}
        P["aff_mark"] = {}
        P["lsem"] = self.S("p_l", "GE")
        P["wsem2"] = [self.S("p_w0", "GE"), self.S("p_w1", "GE")]
        self.weff_done = P["wsem2"]
        P["sve_use"] = {}
        P["st_mark"] = {}
        P["lbT_mark"] = 0
        P["lora_done"] = {}
        self.wait(self.GE, self.cast16, self.cast_done)
        return P

    def prep_step(self, P, step):
        self.ctx = "prep%d" % step
        """One half-weight step: loads, affine, scale, lora mms, store.
        At half==0 also loads la/lb for this k and builds lbT (TensorE
        transposes through the bank-6/7 psum region, bf16 view)."""
        em = self.em
        TE, SE, VE, GE = self.TE, self.SE, self.VE, self.GE
        k, quar = step // 4, step % 4
        buf = step % 2
        qi, waff, sc = P["qi"][buf], P["waff"][buf], P["sc"][buf]
        r0, r1 = quar * 256, quar * 256 + 256

        def emit_load(t):
            if t >= 24:
                return
            tb = t % 2
            tk, tq = t // 4, t % 4
            tr0 = tq * 256
            if t >= 2:
                self.wait(GE, self.sse, P["aff_mark"].get(t - 2, 0))
                self.wait(GE, self.sve, P["sve_use"].get(t - 2, 0))
                self.wait(GE, P["wsem2"][tb], P["st_mark"].get(t - 2, 0))
            P["qsem2"][tb].dinc(em(GE, "dma_start", P["qi"][tb][:, :, :],
                                   self.qw[tk, tr0:tr0 + 256, :].rearrange(
                                       "(c p) d -> p c d", p=128)))
            P["qmk"][t] = P["qsem2"][tb].dinc(
                em(GE, "dma_start", P["sc"][tb][:, :, :],
                   self.qs[tk, tr0:tr0 + 256, :].rearrange(
                       "(c p) f -> p c f", p=128)))

        if step == 0:
            emit_load(0)
        emit_load(step + 1)
        if quar == 0:
            self.wait(GE, self.ste, P["lora_done"].get(k - 1, 0))
            P["lsem"].dinc(em(GE, "dma_start", P["lab"][:, :],
                              self.la[k, :, :]))
            P["lsem"].dinc(em(GE, "dma_start", P["lbb"][:, :, :],
                              self.scr_lb[k, :, :].rearrange(
                                  "(c p) r -> p c r", p=128)))
            psb = self.ps.bitcast(BF16)
            self.wait(TE, P["lsem"])
            self.wait(TE, self.sse, P["lbT_mark"])
            for c in range(NCH):
                mm = em(TE, "matmul",
                        psb[0:32, 6144 + c * 128: 6144 + (c + 1) * 128],
                        P["lbb"][:, c, :], self.identb[:, :],
                        start=True, stop=True, is_transpose=True)
            self.ste.inc(mm)
            self.wait(SE, self.ste)
            P["lbT_mark"] = self.sse.inc(
                em(SE, "activation", P["lbT"][:, :], psb[0:32, 6144:7168],
                   AF.Copy))
        self.wait(SE, P["qsem2"][buf], P["qmk"][step])
        if step >= 2:
            self.wait(SE, self.sve, P["sve_use"].get(step - 2, 0))
        a = em(SE, "activation", waff[:, :, :], qi[:, :, :],
               AF.Copy, bias=-1.0, scale=1.0 / 7.5)
        P["aff_mark"][step] = self.sse.inc(a)
        scb = sc[:, :, :, None].broadcast_to((128, 2, D // 16, 16))
        self.wait(VE, self.sse)
        v = em(VE, "tensor_tensor", waff[:, :, :], waff[:, :, :],
               scb, ALU.mult)
        self.sve.inc(v)
        em(VE, "drain")
        self.wait(TE, self.sse, P["lbT_mark"])
        for idx in range(4):
            oc, dt_ = idx // 2, idx % 2
            go = 2 * quar + oc
            b = 4 + idx % 2
            self.wait_bank(TE, b)
            mm = em(TE, "matmul", self.bank(b),
                    P["lbT"][:, go * 128:(go + 1) * 128],
                    P["lab"][:, dt_ * 512:(dt_ + 1) * 512],
                    start=True, stop=True)
            self.ste.inc(mm)
            self.wait(VE, self.ste)
            v = em(VE, "tensor_tensor",
                   waff[:, oc, dt_ * 512:(dt_ + 1) * 512],
                   waff[:, oc, dt_ * 512:(dt_ + 1) * 512],
                   self.bank(b), ALU.add)
            self.bank_mark[b] = (self.sve, self.sve.inc(v))
        P["lora_done"][k] = self.ste.n
        P["sve_use"][step] = self.sve.n
        self.wait(GE, self.sve, P["sve_use"][step])
        P["st_mark"][step] = P["wsem2"][step % 2].dinc(
            em(GE, "dma_start",
               self.scr_weff[k, r0:r1, :].rearrange("(c p) d -> p c d", p=128),
               waff[:, :, :]))

    def stats_mms(self, xb_fn, xsq_fn, tau_w=TAU):
        em, TE = self.em, self.TE
        self.wait(TE, self.sve, self.statfin_ve)
        self.wait(TE, self.sse)
        for c in range(NCH):
            em(TE, "matmul", self.bank1(3, tau_w), self.ones[:, :],
               xb_fn(c), start=(c == 0), stop=(c == NCH - 1))
        for c in range(NCH):
            mm = em(TE, "matmul", self.bank1(2, tau_w), self.ones[:, :],
                    xsq_fn(c), start=(c == 0), stop=(c == NCH - 1))
        self.ste.inc(mm)
        return self.ste.n

    def stats_copy(self, tau, tau_w=TAU):
        em, VE = self.em, self.VE
        self.wait(VE, self.ste)
        em(VE, "tensor_copy", self.stats[0:1, 0, tau * tau_w:(tau + 1) * tau_w],
           self.bank1(3, tau_w))
        v = em(VE, "tensor_copy",
               self.stats[0:1, 1, tau * tau_w:(tau + 1) * tau_w],
               self.bank1(2, tau_w))
        self.statfin_ve = self.sve.inc(v)
        return self.statfin_ve

    def fin_full(self, ntau=NTAU):
        em, VE, SE = self.em, self.VE, self.SE
        em(VE, "drain")
        for tau in range(ntau):
            m_ap = self.stats[0:1, 0, tau * TAU:(tau + 1) * TAU]
            rs_ap = self.stats[0:1, 1, tau * TAU:(tau + 1) * TAU]
            em(VE, "tensor_tensor", self.fscr[:, :], m_ap, m_ap, ALU.mult)
            em(VE, "drain")
            em(VE, "scalar_tensor_tensor", rs_ap, self.fscr[:, :], -1.0,
               rs_ap, ALU.mult, ALU.add)
            em(VE, "drain")
        rs_full = self.stats[0:1, 1, 0:ntau * TAU]
        em(VE, "tensor_scalar", rs_full, rs_full, EPS, None, ALU.add)
        self.sve.inc(em(VE, "drain"))
        self.wait(SE, self.sve)
        em(SE, "activation", rs_full, rs_full, AF.Ln)
        em(SE, "drain")
        self.sse.inc(em(SE, "activation", rs_full, rs_full, AF.Exp, scale=-0.5))
        return self.sse.n

    def stats_fin(self, dst_m, dst_rs, tau_w=TAU):
        em, VE, SE = self.em, self.VE, self.SE
        self.wait(VE, self.ste)
        em(VE, "tensor_copy", dst_m, self.bank1(3, tau_w))
        v = em(VE, "tensor_scalar", dst_rs, self.bank1(2, tau_w), EPS, None,
               ALU.add)
        self.statfin_ve = self.sve.inc(v)
        em(VE, "drain")
        em(VE, "tensor_tensor", self.fscr[:, 0:tau_w], dst_m, dst_m,
           ALU.mult)
        em(VE, "drain")
        em(VE, "scalar_tensor_tensor", dst_rs, self.fscr[:, 0:tau_w], -1.0,
           dst_rs, ALU.mult, ALU.add)
        self.sve.inc(em(VE, "drain"))
        self.wait(SE, self.sve)
        em(SE, "activation", dst_rs, dst_rs, AF.Ln)
        em(SE, "drain")
        self.sse.inc(em(SE, "activation", dst_rs, dst_rs, AF.Exp, scale=-0.5))
        return self.sse.n

    def bcast(self, m_ap, rs_ap, tau_w=TAU):
        em, TE = self.em, self.TE
        self.wait(TE, self.sse)
        self.wait(TE, self.sve, self.bcast_ve)
        em(TE, "matmul", self.bank(6, tau_w), self.onesr[:, :], m_ap,
           start=True, stop=True)
        mm = em(TE, "matmul", self.bank(7, tau_w), self.onesr[:, :], rs_ap,
                start=True, stop=True)
        self.ste.inc(mm)
        return self.ste.n

    def ln_apply(self, src_f32, dst_bf16, tmp_f32):
        em, VE = self.em, self.VE
        self.wait(VE, self.ste)
        for c in range(NCH):
            em(VE, "scalar_tensor_tensor", tmp_f32[:, c, :], self.bank(6),
               -1.0, src_f32[:, c, :], ALU.mult, ALU.add)
        em(VE, "drain")
        for c in range(NCH):
            v = em(VE, "tensor_tensor", dst_bf16[:, c, :], tmp_f32[:, c, :],
                   self.bank(7), ALU.mult)
        self.bcast_ve = self.sve.inc(v)
        return self.sve.n

    def load_wT(self, slot_ap, scr_ap, sem, kchunks=NCH):
        em, SY = self.em, self.SY
        for kc in range(kchunks):
            sem.dinc(em(SY, "dma_start_transpose", slot_ap[:, kc, :],
                        scr_ap[:, kc * 128:(kc + 1) * 128]))
        return sem.n

    def proj(self, wslot, src, tau, evac, tau_w=TAU, src_tau=None,
             banks=(0, 1)):
        em, TE, SE = self.em, self.TE, self.SE
        if src_tau is None:
            src_tau = tau
        for oc in range(NCH):
            b = banks[oc % len(banks)]
            self.wait_bank(TE, b)
            for kc in range(NCH):
                mm = em(TE, "matmul", self.bank(b, tau_w),
                        wslot[:, kc, oc * 128:(oc + 1) * 128],
                        src[:, kc, src_tau * tau_w:(src_tau + 1) * tau_w],
                        start=(kc == 0), stop=(kc == NCH - 1))
            self.ste.inc(mm)
            self.wait(SE, self.ste)
            s = evac(oc, self.bank(b, tau_w))
            self.bank_mark[b] = (self.sse, self.sse.inc(s))
        return self.sse.n

    def stage0(self):
        nc, em = self.nc, self.em
        self.ctx = "stage0"
        TE, SE, VE, GE, SY = self.TE, self.SE, self.VE, self.GE, self.SY
        with ExitStack() as ph:
            sb = lambda *a: ph.enter_context(nc.sbuf_tensor(*a))
            xt = sb("s0_xt", [128, 2, D], F32)
            r0 = sb("s0_r0", [128, NCH, T], F32)
            tmpf = sb("s0_tmp", [128, NCH, TAU], F32)
            rb = sb("s0_rb", [128, NCH, TAU], BF16)
            rsq = sb("s0_rsq", [128, NCH, TAU], BF16)
            xsem = [self.S("s0x0", "SY"), self.S("s0x1", "SY")]
            ld_marks = []
            self.wait(SY, self.sgp)
            self.wait(TE, self.sve, self.ident_ve)
            for st in range(2):
                xsem[st % 2].dinc(em(SY, "dma_start", xt[:, st, :],
                                     self.x_in[st * 128:(st + 1) * 128, :]))
                ld_marks.append(xsem[st % 2].n)
            evac_mark = {}
            for tau in range(NTAU):
                for st in range(NSUB):
                    gst = tau * NSUB + st
                    buf = gst % 2
                    bb = (6, 4, 0)[gst % 3]
                    self.wait(TE, xsem[gst % 2], ld_marks[gst])
                    if gst >= 3:
                        self.wait(TE, self.sse, evac_mark[gst - 3])
                    if bb == 6:
                        self.wait(TE, self.sve, self.bcast_ve)
                    for c in range(NCH):
                        mm = em(TE, "matmul",
                                self.ps[:, 512 * bb + 128 * c:
                                        512 * bb + 128 * (c + 1)],
                                xt[:, buf, 128 * c:128 * (c + 1)],
                                self.ident[:, :], start=True, stop=True,
                                is_transpose=True)
                    te_mark = self.ste.inc(mm)
                    if gst + 2 < T // 128:
                        self.wait(SY, self.ste, te_mark)
                        xsem[buf].dinc(em(SY, "dma_start", xt[:, buf, :],
                                          self.x_in[(gst + 2) * 128:(gst + 3) * 128, :]))
                        ld_marks.append(xsem[buf].n)
                    self.wait(SE, self.ste, te_mark)
                    src = self.ps[:, 512 * bb: 512 * bb + 1024].rearrange(
                        "p (c f) -> p c f", f=128)
                    off = tau * TAU + 128 * st
                    em(SE, "copy", r0[:, :, off:off + 128], src)
                    em(SE, "activation", rb[:, :, 128 * st:128 * (st + 1)],
                       src, AF.Copy)
                    sI = em(SE, "activation", rsq[:, :, 128 * st:128 * (st + 1)],
                            src, AF.Square)
                    evac_mark[gst] = self.sse.inc(sI)
                self.stats_mms(lambda c: rb[:, c, :], lambda c: rsq[:, c, :])
                m_ap = self.stats[0:1, 0, tau * TAU:(tau + 1) * TAU]
                rs_ap = self.stats[0:1, 1, tau * TAU:(tau + 1) * TAU]
                self.stats_fin(m_ap, rs_ap)
                self.wait(SE, self.ste)
                self.wait(GE, self.sse, evac_mark[tau * NSUB + NSUB - 1])
                self.g16.dinc(em(GE, "dma_start",
                                 self.scr_r[0][tau, :, :, :],
                                 r0[:, :, tau * TAU:(tau + 1) * TAU]))
                self.bcast(m_ap, rs_ap)
                self.ln_apply(r0[:, :, tau * TAU:(tau + 1) * TAU],
                              self.XN[:, :, tau * TAU:(tau + 1) * TAU], tmpf)
        self.barrier()

    def attn_a(self):
        nc, em = self.nc, self.em
        self.ctx = "attn_a"
        TE, SE, VE, GE, SY = self.TE, self.SE, self.VE, self.GE, self.SY
        with ExitStack() as ph:
            sb = lambda *a: ph.enter_context(nc.sbuf_tensor(*a))
            x1b = sb("aa_x1b", [128, NCH, T], BF16)
            ws = self.aws
            rrd = self.S("aa_rrd", "SY")
            wp3s = sb("aa_wp3", [128, NCH, D], BF16)
            m_wp2, m_wp1 = self.aw_marks

            ph1 = ExitStack()
            x2c = ph1.enter_context(nc.sbuf_tensor("aa_x2c", [128, NCH, T], BF16))
            m_conv = self.load_wT(wp3s, self.scr["conv_w"], self.wsld[2])
            self.emit_castA2()
            phA = ExitStack()
            x2b = phA.enter_context(nc.sbuf_tensor("aa_x2b", [128, NCH, T], BF16))
            with ExitStack() as phP:
                P = self.prep_alloc(x2c)
                nstep = [0]

                def drive_prep(slots_done):
                    want = min(24, (slots_done * 24 + 7) // 8)
                    while nstep[0] < want:
                        self.prep_step(P, nstep[0])
                        nstep[0] += 1

                slot = [0]
                self.wait(TE, self.wsld[0], m_wp2)
                for tau in range(NTAU):
                    self.proj(ws[0], self.XN, tau,
                              lambda oc, pb, _t=tau: em(
                                  SE, "activation",
                                  x2b[:, oc, _t * TAU:(_t + 1) * TAU], pb,
                                  AF.Identity, bias=self.cvc("bp2", oc)))
                    slot[0] += 1
                    drive_prep(slot[0])
                self.wait(TE, self.wsld[1], m_wp1)
                for tau in range(NTAU):
                    self.proj(ws[1], self.XN, tau,
                              lambda oc, pb, _t=tau: em(
                                  SE, "activation",
                                  x1b[:, oc, _t * TAU:(_t + 1) * TAU], pb,
                                  AF.Gelu, bias=self.cvc("bp1", oc)))
                    slot[0] += 1
                    drive_prep(slot[0])
            if True:
                self.wait(TE, self.wsld[2], m_conv)
                self.wait(SY, self.ste)
                self.wait(SY, self.cstA2, self.castA2_done)
                m_wa = self.load_wT(ws[0], self.scr["Wa"], self.wsld[0])
                m_wx = self.load_wT(ws[1], self.scr["Wx"], self.wsld[1])
                self.wait(SE, self.sve, P["sve_use"][23])
                for _w in P["wsem2"]:
                    self.wait(SE, _w)
                for tau in range(NTAU):
                    self.proj(wp3s, x2b, tau,
                              lambda oc, pb, _t=tau: em(
                                  SE, "activation",
                                  x2c[:, oc, _t * TAU:(_t + 1) * TAU], pb,
                                  AF.Relu, bias=self.cvc("conv_b", oc)),
                              banks=(0, 1, 4, 5))
                conv_te_done = self.ste.n
                phA.close()

            with ExitStack() as phB:
                NB = 3
                gsc = phB.enter_context(
                    nc.sbuf_tensor("aa_gsc", [128, NB, T], F32))
                tf = phB.enter_context(
                    nc.sbuf_tensor("aa_tf", [128, T], F32))
                rg = phB.enter_context(
                    nc.sbuf_tensor("aa_rg", [128, T], BF16))
                gscC = wp3s.rearrange("p c d -> p (c d)").rearrange(
                    "p (a b) -> p a b", a=4)[:, 0:NB, :]
                self.wait(TE, self.wsld[0], m_wa)
                self.wait(TE, self.wsld[1], m_wx)
                prod_done = 0
                aexp_mark = {}

                def emit_gates(c):
                    self.ctx = "a3.gates%d" % c
                    """TE gate matmuls + SE sigmoid evacs + a = exp."""
                    bufc = c % NB
                    em(SE, "drain")
                    for kind, wslot, coff, b0, dstT in (
                            (0, ws[0], "ba", 0, None),
                            (1, ws[1], "bx", 2, gscC)):
                        for ft in range(NTAU):
                            b = b0 + ft % 2
                            self.wait_bank(TE, b)
                            for kc in range(NCH):
                                mm = em(TE, "matmul", self.bank(b),
                                        wslot[:, kc, c * 128:(c + 1) * 128],
                                        x2c[:, kc, ft * TAU:(ft + 1) * TAU],
                                        start=(kc == 0), stop=(kc == NCH - 1))
                            self.ste.inc(mm)
                            self.wait(SE, self.ste)
                            if kind == 1 and ft == 0 and c >= NB:
                                self.wait(SE, self.sve, self._p2done.get(c - NB, 0))
                            if kind == 0:
                                dst = rg[:, ft * TAU:(ft + 1) * TAU]
                            else:
                                dst = dstT[:, bufc, ft * TAU:(ft + 1) * TAU]
                            sI = em(SE, "activation", dst, self.bank(b),
                                    AF.Sigmoid, bias=self.cvc(coff, c))
                            self.bank_mark[b] = (self.sse, self.sse.inc(sI))
                    em(SE, "drain")
                    if c >= NB:
                        self.wait(SE, self.sve, self._p2done.get(c - NB, 0))
                    sI = em(SE, "activation", gsc[:, bufc, :], rg[:, :],
                            AF.Exp, scale=self.cvc("nsplam", c))
                    aexp_mark[c] = self.sse.inc(sI)

                def emit_chain(c):
                    self.ctx = "a3.chain%d" % c
                    """t = clamp(1-a^2); b = exp(.5 ln t); g = ig*x2c*b;
                    ys = scan(a, g); prod *= ys.  Tf is the fp32 work slot."""
                    bufc = c % NB
                    A = gsc[:, bufc, :]
                    C = gscC[:, bufc, :]
                    self.wait(VE, self.sse, aexp_mark[c])
                    em(VE, "drain")
                    em(VE, "tensor_tensor", tf[:, :], A, A, ALU.mult)
                    em(VE, "drain")
                    em(VE, "tensor_scalar", tf[:, :], tf[:, :], -1.0, 1.0,
                       ALU.mult, ALU.add)
                    em(VE, "drain")
                    em(VE, "tensor_scalar", tf[:, :], tf[:, :], 1e-30, None,
                       ALU.max)
                    em(VE, "drain")
                    v = em(VE, "tensor_tensor", C, C, x2c[:, c, :], ALU.mult)
                    vm = self.sve.inc(v)
                    self.wait(SE, self.sve, vm)
                    em(SE, "drain")
                    em(SE, "activation", tf[:, :], tf[:, :], AF.Ln)
                    em(SE, "drain")
                    sI = em(SE, "activation", tf[:, :], tf[:, :], AF.Exp,
                            scale=0.5)
                    sm = self.sse.inc(sI)
                    self.wait(VE, self.sse, sm)
                    em(VE, "tensor_tensor", C, C, tf[:, :], ALU.mult)
                    em(VE, "drain")
                    em(VE, "tensor_tensor_scan", tf[:, :], A, C, 0.0,
                       ALU.mult, ALU.add)
                    em(VE, "drain")
                    v = em(VE, "tensor_tensor", x1b[:, c, :], x1b[:, c, :],
                           tf[:, :], ALU.mult)
                    self._p2done[c] = self.sve.inc(v)
                    return self._p2done[c]

                self._p2done = {}
                for c in range(NCH):
                    emit_gates(c)
                    if c >= 2:
                        emit_chain(c - 2)
                prod_done6 = emit_chain(NCH - 2)
                prod_done = emit_chain(NCH - 1)
            self.wait(SY, self.ste)
            self.wait(SY, self.cstA2, self.castA2_done)
            m_wp3 = self.load_wT(ws[0], self.scr["Wp3"], self.wsld[2])
            self.emit_castB()
            ph1.close()
            prod_marks = dict(self._p2done)

            with ExitStack() as phC:
                sbC = lambda *a: phC.enter_context(nc.sbuf_tensor(*a))
                rtile = sbC("aa_rt", [128, NCH, TAU], F32)
                otile = sbC("aa_ot", [128, 2, NCH, TAU], F32)
                rb = sbC("aa_rb", [128, NCH, TAU], BF16)
                rsq = sbC("aa_rsq", [128, NCH, TAU], BF16)
                self.wait(TE, self.wsld[2], m_wp3)
                self.wait(SY, self.sve, prod_done)
                r1_done = {}
                spsem = [self.S("aa_sp0", "GE"), self.S("aa_sp1", "GE")]
                spmark = {}

                def a4_stats(tau):
                    self.ctx = "a4.stats%d" % tau
                    pp = tau % 2
                    self.wait(SE, self.sve, r1_done[tau])
                    self.wait(SE, self.ste)
                    sI = em(SE, "activation", rb[:, :, :],
                            otile[:, pp, :, :], AF.Copy)
                    self.sse.inc(sI)
                    em(VE, "drain")
                    v = em(VE, "tensor_tensor", rsq[:, :, :],
                           otile[:, pp, :, :], otile[:, pp, :, :], ALU.mult)
                    self.sve.inc(v)
                    self.wait(TE, self.sve)
                    self.stats_mms(lambda c: rb[:, c, :],
                                   lambda c: rsq[:, c, :])
                    self.stats_fin(self.stats[0:1, 0, tau * TAU:(tau + 1) * TAU],
                                   self.stats[0:1, 1, tau * TAU:(tau + 1) * TAU])
                    self.wait(GE, self.sve, r1_done[tau])
                    spmark[tau] = spsem[pp].dinc(
                        em(GE, "dma_start", self.scr_r[1][tau, :, :, :],
                           otile[:, pp, :, :]))

                def proj_split(tau, pp):
                    self.ctx = "a4.proj%d" % tau
                    """Wp3 projection with k 0..5 gated only on chain(5),
                    k 6..7 gated on the final chains -- lets A4 start while
                    the last two gate chains still run on VE/SE."""
                    for oc in range(NCH):
                        b = (0, 1, 4, 5)[oc % 4]
                        self.wait_bank(TE, b)
                        if oc == 0:
                            self.wait(TE, self.sve, prod_marks[5])
                        for kc in range(6):
                            em(TE, "matmul", self.bank(b),
                               ws[0][:, kc, oc * 128:(oc + 1) * 128],
                               x1b[:, kc, tau * TAU:(tau + 1) * TAU],
                               start=(kc == 0), stop=False)
                        if oc == 0:
                            self.wait(TE, self.sve, prod_marks[7])
                        for kc in (6, 7):
                            mm = em(TE, "matmul", self.bank(b),
                                    ws[0][:, kc, oc * 128:(oc + 1) * 128],
                                    x1b[:, kc, tau * TAU:(tau + 1) * TAU],
                                    start=False, stop=(kc == 7))
                        self.ste.inc(mm)
                        self.wait(SE, self.ste)
                        sI = em(SE, "activation", otile[:, pp, oc, :],
                                self.bank(b), AF.Identity,
                                bias=self.cvc("bp3", oc))
                        self.bank_mark[b] = (self.sse, self.sse.inc(sI))

                for tau in range(NTAU):
                    pp = tau % 2
                    if tau >= 1:
                        self.wait(SY, self.sve)
                    rrd.dinc(em(SY, "dma_start", rtile[:, :, :],
                                self.scr_r[0][tau, :, :, :]))
                    if tau >= 2:
                        self.wait(SE, spsem[tau % 2], spmark[tau - 2])
                    if tau <= 1:
                        proj_split(tau, pp)
                    else:
                        self.proj(ws[0], x1b, tau,
                                  lambda oc, pb, _pp=pp: em(SE, "activation",
                                                    otile[:, _pp, oc, :], pb,
                                                    AF.Identity,
                                                    bias=self.cvc("bp3", oc)),
                                  banks=(0, 1, 4, 5))
                    self.wait(VE, rrd)
                    self.wait(VE, self.sse)
                    for c in range(NCH):
                        v = em(VE, "tensor_tensor", otile[:, pp, c, :],
                               otile[:, pp, c, :], rtile[:, c, :], ALU.add)
                    r1_done[tau] = self.sve.inc(v)
                    if tau >= 1:
                        a4_stats(tau - 1)
                a4_stats(NTAU - 1)
        self.barrier()

    def ffn(self, ri, produce_xn, final_out):
        self.fctx = "ffn%d" % ri
        """FFN instance with ti-level depth-1 software pipeline within each
        half: TE runs q1(t0), q1(t1)+stats(t0), bcast(t0), q2(t0)+stats(t1),
        bcast(t1), q2(t1) while the LN chains run on VE/SE in the gaps.
        q2 output is added to h directly from PSUM on VE (with per-partition
        bias via scalar_tensor_tensor)."""
        nc, em = self.nc, self.em
        TE, SE, VE, GE, SY = self.TE, self.SE, self.VE, self.GE, self.SY
        with ExitStack() as ph:
            sb = lambda n, *a: ph.enter_context(
                nc.sbuf_tensor(n + "_i%d" % ri, *a))
            H = sb("f_h", [128, NCH, HALF], F32)
            rtile = sb("f_rt", [128, NCH, TAU], F32)
            ws = [sb("f_ws%d" % i, [128, NCH, D], BF16) for i in range(3)]
            y1b = sb("f_y1b", [128, 2, NCH, TAU], BF16)
            y2b = sb("f_y2b", [128, 2, NCH, TAU], BF16)
            tmpb = sb("f_tmpb", [128, NCH, TAU], BF16)
            otm = None
            if not final_out:
                rb = y1b[:, 0]
            rrd = self.S("f_rrd%d" % ri, "SY")
            rrds = [rrd, self.S("f_rrd2%d" % ri, "SY")]
            rt2 = y2b.bitcast(F32).rearrange("p a c t -> p (a c t)")
            rt2 = rt2.rearrange("p (c t) -> p c t", c=NCH)
            rts = [rtile, rt2]

            def q1(ti, k0, wslot, hb, ocs=range(NCH)):
                self.ctx = self.fctx + ".q1(%d)" % ti
                """TE q1-mms for the given ocs; SE evac -> y1b[ti%2] (+bias)."""
                p = ti % 2
                for oc in ocs:
                    b = (0, 1, 4, 5)[oc % 4]
                    self.wait_bank(TE, b)
                    for kc in range(NCH):
                        mm = em(TE, "matmul", self.bank(b),
                                wslot[:, kc, oc * 128:(oc + 1) * 128],
                                hb[:, kc, ti * TAU:(ti + 1) * TAU],
                                start=(kc == 0), stop=(kc == NCH - 1))
                    self.ste.inc(mm)
                    self.wait(VE, self.ste)
                    v = em(VE, "tensor_scalar", y1b[:, p, oc, :], self.bank(b),
                           self.cvc("qb", k0 * 8 + oc), None, ALU.add)
                    self.bank_mark[b] = (self.sve, self.sve.inc(v))

            sqf = sb("f_sqf", [128, NCH, TAU], BF16)

            def q1_sq_se(ti):
                p = ti % 2
                self.wait(VE, self.ste)
                em(VE, "drain")
                v = em(VE, "tensor_tensor", sqf[:, :, :], y1b[:, p, :, :],
                       y1b[:, p, :, :], ALU.mult)
                self.sve.inc(v)

            def q1_sq_te(ti):
                self.ctx = self.fctx + ".stats(%d)" % ti
                """TE: stats mms block over y1b + sqf."""
                p = ti % 2
                self.wait(TE, self.sve, self.statfin_ve)
                self.wait(TE, self.sve)
                for oc in range(NCH):
                    em(TE, "matmul", self.bank1(3), self.ones[:, :],
                       y1b[:, p, oc, :], start=(oc == 0), stop=(oc == NCH - 1))
                for oc in range(NCH):
                    mm = em(TE, "matmul", self.bank1(2), self.ones[:, :],
                            sqf[:, oc, :], start=(oc == 0),
                            stop=(oc == NCH - 1))
                self.ste.inc(mm)

            sti_t = [self.stats_i, None]
            mrs = sb("f_mrs", [128, 2, 2, TAU], BF16)

            def stats_tail(ti):
                p = ti % 2
                self.wait(VE, self.ste)
                em(VE, "tensor_copy", sti_t[p][0:1, 0, :], self.bank1(3))
                v = em(VE, "tensor_scalar", sti_t[p][0:1, 1, :], self.bank1(2),
                       EPS, None, ALU.add)
                self.statfin_ve = self.sve.inc(v)

            def fin_i(ti):
                p = ti % 2
                m_ap = sti_t[p][0:1, 0, :]
                rs_ap = sti_t[p][0:1, 1, :]
                scr = sti_t[1 - p][0:1, 0, :]
                self.wait(VE, self.ste)
                em(VE, "drain")
                em(VE, "tensor_tensor", scr, m_ap, m_ap, ALU.mult)
                em(VE, "drain")
                em(VE, "scalar_tensor_tensor", rs_ap, scr, -1.0,
                   rs_ap, ALU.mult, ALU.add)
                self.sve.inc(em(VE, "drain"))
                self.wait(SE, self.sve)
                em(SE, "activation", rs_ap, rs_ap, AF.Ln)
                em(SE, "drain")
                sI = em(SE, "activation", rs_ap, rs_ap, AF.Exp, scale=-0.5)
                return m_ap, rs_ap, self.sse.inc(sI)

            def bcast_evac(ti, bmk=None):
                p = ti % 2
                self.wait(VE, self.ste, bmk)
                em(VE, "tensor_copy", mrs[:, 0, p, :], self.bank(6))
                v = em(VE, "tensor_copy", mrs[:, 1, p, :], self.bank(7))
                self.bcast_ve = self.sve.inc(v)

            def apply_gelu(ti, sbk):
                p = ti % 2
                if final_out:
                    self.wait(VE, self.o16)
                em(VE, "drain")
                for c in range(NCH):
                    em(VE, "tensor_tensor", tmpb[:, c, :], y1b[:, p, c, :],
                       mrs[:, 0, p, :], ALU.subtract)
                em(VE, "drain")
                for c in range(NCH):
                    v = em(VE, "tensor_tensor", tmpb[:, c, :],
                           tmpb[:, c, :], mrs[:, 1, p, :], ALU.mult)
                self.sve.inc(v)
                self.wait(SE, self.sve)
                for c in range(NCH):
                    sI = em(SE, "activation", y2b[:, p, c, :],
                            tmpb[:, c, :], AF.Gelu,
                            scale=self.cvc("lng", sbk * 8 + c),
                            bias=self.cvc("lnb", sbk * 8 + c))
                self.sse.inc(sI)

            def q2_hadd(ti, k1, wslot, ocs=range(NCH)):
                self.ctx = self.fctx + ".q2(%d)" % ti
                """TE q2-mms; VE adds psum (+bias) into H directly."""
                p = ti % 2
                hs = H[:, :, ti * TAU:(ti + 1) * TAU]
                for oc in ocs:
                    b = (0, 1, 4, 5)[oc % 4]
                    self.wait_bank(TE, b)
                    for kc in range(NCH):
                        mm = em(TE, "matmul", self.bank(b),
                                wslot[:, kc, oc * 128:(oc + 1) * 128],
                                y2b[:, p, kc, :],
                                start=(kc == 0), stop=(kc == NCH - 1))
                    self.ste.inc(mm)
                    self.wait(VE, self.ste)
                    v = em(VE, "scalar_tensor_tensor", hs[:, oc, :],
                           self.bank(b), self.cvc("qb", k1 * 8 + oc),
                           hs[:, oc, :], ALU.add, ALU.add)
                    self.bank_mark[b] = (self.sve, self.sve.inc(v))
                return self.sve.n

            for _w in self.weff_done:
                self.wait(SY, _w)
            wload_mark = {}
            wuse_mark = {}
            nextk = [0]

            def load_next_weff(slot):
                k = nextk[0]
                if k >= 12:
                    return
                self.wait(SY, self.ste, wuse_mark.get(slot, 0))
                m = self.load_wT(ws[slot], self.scr_weff[k % 6, :, :],
                                 self.wsld[3 + slot])
                wload_mark[k] = (slot, m)
                nextk[0] += 1

            self.wait(SY, self.sve)
            self.wait(SY, self.sse)
            self.wait(SY, self.ste)
            self.wait(SY, self.g16)
            hmk0 = {}
            for ti in range(NH):
                hmk0[ti] = rrds[ti % 2].dinc(
                    em(SY, "dma_start", rts[ti % 2][:, :, :],
                       self.scr_r[ri][ti, :, :, :]))
            load_next_weff(0)
            load_next_weff(1)
            load_next_weff(2)
            omk = {0: 0, 1: 0}
            for half in range(2):
                t0 = half * NH
                self.fctx = "ffn%d.h%d" % (ri, half)
                self.ctx = self.fctx + ".hinit"
                sti_t[1] = self.stats[0:1, :, (t0 + 1) * TAU:(t0 + 2) * TAU]
                hb = self.XN[:, :, half * HALF:(half + 1) * HALF]
                hb_mk = {}
                if half == 0:
                    hmk = hmk0
                else:
                    self.wait(SY, self.sve)
                    self.wait(SY, self.sse)
                    self.wait(SY, self.ste)
                    self.wait(SY, self.g16)
                    hmk = {}
                    for ti in range(NH):
                        hmk[ti] = rrds[ti % 2].dinc(
                            em(SY, "dma_start", rts[ti % 2][:, :, :],
                               self.scr_r[ri][(t0 + ti), :, :, :]))
                for ti in range(NH):
                    tau = t0 + ti
                    rtl = rts[ti % 2]
                    m_ap = self.stats[0:1, 0, tau * TAU:(tau + 1) * TAU]
                    rs_ap = self.stats[0:1, 1, tau * TAU:(tau + 1) * TAU]
                    self.bcast(m_ap, rs_ap)
                    self.wait(VE, rrds[ti % 2], hmk[ti])
                    hs = H[:, :, ti * TAU:(ti + 1) * TAU]
                    self.wait(VE, self.ste)
                    for c in range(NCH):
                        em(VE, "scalar_tensor_tensor", hs[:, c, :], self.bank(6),
                           -1.0, rtl[:, c, :], ALU.mult, ALU.add)
                    em(VE, "drain")
                    for c in range(NCH):
                        v = em(VE, "tensor_tensor", hs[:, c, :], hs[:, c, :],
                               self.bank(7), ALU.mult)
                    self.bcast_ve = self.sve.inc(v)
                    em(VE, "drain")
                    v = em(VE, "tensor_copy", hb[:, :, ti * TAU:(ti + 1) * TAU],
                           hs)
                    hb_mk[ti] = self.sve.inc(v)
                for sbk in range(3):
                    self.fctx = self.fctx.split(".sb")[0] + ".sb%d" % sbk
                    self.ctx = self.fctx
                    k0, k1 = 2 * sbk, 2 * sbk + 1
                    kk = half * 6 + 2 * sbk
                    s0, m0l = wload_mark[kk]
                    s1, m1l = wload_mark[kk + 1]
                    w0slot, w1slot = ws[s0], ws[s1]
                    self.wait(TE, self.wsld[3 + s0], m0l)
                    self.wait(TE, self.sve, hb_mk[0])
                    if final_out and sbk == 0:
                        self.wait(VE, self.o16b)
                    q1(0, k0, w0slot, hb)
                    q1_sq_se(0)
                    self.wait(TE, self.sve, hb_mk[1])
                    q1(1, k0, w0slot, hb, range(0, 4))
                    q1_sq_te(0)
                    stats_tail(0)
                    m0, rs0, f0 = fin_i(0)
                    q1(1, k0, w0slot, hb, range(4, 8))
                    wuse_mark[s0] = self.ste.n
                    load_next_weff(s0)
                    q1_sq_se(1)
                    self.wait(TE, self.sse, f0)
                    self.bcast(m0, rs0)
                    bcast_evac(0)
                    q1_sq_te(1)
                    stats_tail(1)
                    apply_gelu(0, sbk)
                    g0 = self.sse.n
                    m1, rs1, f1 = fin_i(1)
                    self.wait(TE, self.sse, f1)
                    bmk = self.bcast(m1, rs1)
                    bcast_evac(1, bmk)
                    apply_gelu(1, sbk)
                    g1 = self.sse.n
                    self.wait(TE, self.wsld[3 + s1], m1l)
                    self.wait(TE, self.sse, g0)
                    q2_hadd(0, k1, w1slot)
                    if sbk == 2:
                        self.wait(SY, self.sve)
                        self.wait(SY, self.sse)
                        self.wait(SY, self.g16)
                        tmk0 = rrds[0].dinc(
                            em(SY, "dma_start", rtile[:, :, :],
                               self.scr_r[ri][t0, :, :, :]))
                    if sbk < 2:
                        em(VE, "drain")
                        v = em(VE, "tensor_copy", hb[:, :, 0:TAU],
                               H[:, :, 0:TAU])
                        hb_mk[0] = self.sve.inc(v)
                    self.wait(TE, self.sse, g1)
                    h1_done = q2_hadd(1, k1, w1slot)
                    wuse_mark[s1] = self.ste.n
                    load_next_weff(s1)
                    if sbk < 2:
                        em(VE, "drain")
                        v = em(VE, "tensor_copy", hb[:, :, TAU:HALF],
                               H[:, :, TAU:HALF])
                        hb_mk[1] = self.sve.inc(v)
                self.ctx = "ffn%d.h%d.tail" % (ri, half)
                self.wait(SY, self.ste)
                self.wait(SY, self.sse)
                self.wait(SY, self.g16)
                tmk = {0: tmk0,
                       1: rrds[1].dinc(em(SY, "dma_start", rt2[:, :, :],
                                          self.scr_r[ri][t0 + 1, :, :, :]))}
                em(VE, "drain")
                for ti in range(NH):
                    tau = t0 + ti
                    rtl = rts[ti % 2]
                    self.wait(VE, rrds[ti % 2], tmk[ti])
                    hs = H[:, :, ti * TAU:(ti + 1) * TAU]
                    for c in range(NCH):
                        v = em(VE, "tensor_tensor", rtl[:, c, :], hs[:, c, :],
                               rtl[:, c, :], ALU.add)
                    rdone = self.sve.inc(v)
                    if not final_out:
                        self.wait(SE, self.sve, rdone)
                        self.wait(SE, self.ste)
                        sI = em(SE, "activation", rb[:, :, :], rtl[:, :, :],
                                AF.Copy)
                        self.sse.inc(sI)
                        em(VE, "drain")
                        v = em(VE, "tensor_tensor", sqf[:, :, :],
                               rtl[:, :, :], rtl[:, :, :], ALU.mult)
                        self.sve.inc(v)
                        self.wait(TE, self.sve)
                        self.stats_mms(lambda c: rb[:, c, :],
                                       lambda c: sqf[:, c, :])
                        m_ap = self.stats[0:1, 0, tau * TAU:(tau + 1) * TAU]
                        rs_ap = self.stats[0:1, 1, tau * TAU:(tau + 1) * TAU]
                        self.stats_fin(m_ap, rs_ap)
                        self.wait(GE, self.sve, rdone)
                        self.g16.dinc(em(GE, "dma_start",
                                         self.scr_r[ri + 1][tau, :, :, :],
                                         rtl[:, :, :]))
                        if produce_xn:
                            self.bcast(m_ap, rs_ap)
                            self.wait(VE, self.ste)
                            for c in range(NCH):
                                em(VE, "scalar_tensor_tensor", rb[:, c, :],
                                   self.bank(6), -1.0, rtl[:, c, :],
                                   ALU.mult, ALU.add)
                            em(VE, "drain")
                            for c in range(NCH):
                                v = em(VE, "tensor_tensor",
                                       self.XN[:, c, tau * TAU:(tau + 1) * TAU],
                                       rb[:, c, :], self.bank(7), ALU.mult)
                            self.bcast_ve = self.sve.inc(v)
                    else:
                        otms = [tmpb.bitcast(F32).rearrange(
                                    "p a b -> p (a b)")[:, 0:D],
                                y1b.rearrange("p a c t -> p (a c t)"
                                              ).bitcast(F32)[:, 0:D]]
                        osem = [self.o16, self.o16b]
                        for st in range(NSUB):
                            bb = 6 if st % 2 == 0 else 4
                            par = st % 2
                            self.wait(TE, self.sve, rdone)
                            self.wait(TE, self.sse)
                            if bb == 6:
                                self.wait(TE, self.sve, self.bcast_ve)
                            for c in range(NCH):
                                mm = em(TE, "matmul",
                                        self.ps[:, 512 * bb + 128 * c:
                                                512 * bb + 128 * (c + 1)],
                                        rtl[:, c, st * 128:(st + 1) * 128],
                                        self.ident[:, :], start=True, stop=True,
                                        is_transpose=True)
                            self.ste.inc(mm)
                            self.wait(SE, self.ste)
                            self.wait(SE, osem[par], omk[par])
                            sI = em(SE, "copy", otms[par][:, :],
                                    self.ps[:, 512 * bb: 512 * bb + 1024])
                            self.sse.inc(sI)
                            self.wait(SY, self.sse)
                            omk[par] = osem[par].dinc(
                                em(SY, "dma_start",
                                   self.y_out[tau * TAU + st * 128:
                                              tau * TAU + (st + 1) * 128, :],
                                   otms[par][:, :]))
                if not final_out:
                    self.wait(SE, self.g16)
        self.barrier()

    def attn_b(self):
        nc, em = self.nc, self.em
        self.ctx = "attn_b"
        TE, SE, VE, GE, SY = self.TE, self.SE, self.VE, self.GE, self.SY
        with ExitStack() as ph:
            sb = lambda *a: ph.enter_context(nc.sbuf_tensor(*a))
            Bu = sb("b_bu", [128, 2, NKS, TAU], BF16)
            wsB = [sb("b_wsb%d" % i, [128, 2, NCH, SIG], BF16) for i in range(2)]
            wsC = [sb("b_wsc%d" % i, [128, 2, NKS, D], BF16) for i in range(2)]
            wsD = sb("b_wsd", [128, NCH, D], BF16)
            oacc = sb("b_oacc", [128, NH, NCH, TAU], F32)
            rtile = sb("b_rt", [128, NCH, TAU], F32)
            rb = sb("b_rb", [128, NCH, TAU], BF16)
            rsq = sb("b_rsq", [128, NCH, TAU], BF16)
            rrd = self.S("b_rrd", "SY")
            wload = [self.S("b_wl%d" % i, "SY") for i in range(2)]
            dload = self.S("b_dl", "SY")

            self.wait(SY, self.cstB, self.castB_done)

            def load_sigma(buf, sg):
                sem = wload[buf]
                for j, w in enumerate(["B_re", "B_im"]):
                    for kc in range(NCH):
                        sem.dinc(em(SY, "dma_start_transpose",
                                    wsB[buf][:, j, kc, :],
                                    self.scr[w][sg * SIG:(sg + 1) * SIG,
                                                kc * 128:(kc + 1) * 128]))
                for j, w in enumerate(["C_re", "C_im"]):
                    for kc in range(NKS):
                        sem.dinc(em(SY, "dma_start_transpose",
                                    wsC[buf][:, j, kc, :],
                                    self.scr[w][:, sg * SIG + kc * 128:
                                                sg * SIG + (kc + 1) * 128]))
                return sem.n

            nblk = 2 * NSIG
            marks = {}
            marks[0] = load_sigma(0, 0)
            self.load_wT(wsD, self.scr["Dm"], dload)
            marks[1] = load_sigma(1, 1)
            te_done = {}
            bu_guard = self.sse.n
            rrds_b = [rrd, self.S("b_rrd2", "SY")]
            xn_rt = [self.XN[:, :, h * HALF:(h + 1) * HALF].bitcast(F32)
                     for h in range(2)]
            tmk = {}

            def emit_tail_tau(h, ti):
                tau = h * NH + ti
                rtl = rtile if ti == 0 else xn_rt[h]
                self.wait(VE, rrds_b[ti], tmk[(h, ti)])
                em(VE, "drain")
                for c in range(NCH):
                    v = em(VE, "tensor_tensor", rtl[:, c, :],
                           oacc[:, ti, c, :], rtl[:, c, :], ALU.add)
                rdone = self.sve.inc(v)
                self.wait(SE, self.sve, rdone)
                self.wait(SE, self.ste)
                em(SE, "activation", rb[:, :, :], rtl[:, :, :], AF.Copy)
                sq = em(SE, "activation", rsq[:, :, :], rtl[:, :, :],
                        AF.Square)
                self.sse.inc(sq)
                self.wait(TE, self.sse)
                self.stats_mms(lambda c: rb[:, c, :], lambda c: rsq[:, c, :])
                self.stats_copy(tau)
                self.wait(GE, self.sve, rdone)
                self.g16.dinc(em(GE, "dma_start", self.scr_r[3][tau, :, :, :],
                                 rtl[:, :, :]))

            for blki in range(nblk):
                half, sg = blki // NSIG, blki % NSIG
                buf = blki % 2
                if sg == NSIG - 1:
                    self.wait(SY, self.sve)
                    self.wait(SY, self.g16)
                    tmk[(half, 0)] = rrds_b[0].dinc(
                        em(SY, "dma_start", rtile[:, :, :],
                           self.scr_r[2][half * NH, :, :, :]))
                self.wait(TE, wload[buf], marks[blki])
                for ti in range(NH):
                    tau = half * NH + ti
                    sse_base = self.sse.n
                    bm = {}
                    for idx in range(2 * NKS):
                        scl, j = idx // 2, idx % 2
                        sc = sg * NKS + scl
                        b = (4, 5, 6, 7)[idx % 4]
                        self.wait(TE, self.sse, bm.get(idx - 4, bu_guard))
                        for kc in range(NCH):
                            mm = em(TE, "matmul", self.bank(b),
                                    wsB[buf][:, j, kc, scl * 128:(scl + 1) * 128],
                                    self.XN[:, kc, tau * TAU:(tau + 1) * TAU],
                                    start=(kc == 0), stop=(kc == NCH - 1))
                        self.ste.inc(mm)
                        self.wait(SE, self.ste)
                        if idx == 0:
                            self.wait(SE, self.ste, te_done.get("out", 0))
                        sc_n = "g2" if j == 0 else "g2n"
                        bi_n = "cre2" if j == 0 else "cim2n"
                        s = em(SE, "activation", Bu[:, j, scl, :], self.bank(b),
                               AF.Identity, scale=self.cvc(sc_n, sc),
                               bias=self.cvc(bi_n, sc))
                        bm[idx] = self.sse.inc(s)
                    bu_guard = self.sse.n
                    if half == 1 and sg == 0:
                        emit_tail_tau(0, ti)
                    if half == 1 and sg == NSIG - 1 and ti == 1:
                        emit_tail_tau(1, 0)
                    if sg == 0:
                        self.wait(TE, dload, dload.n)
                    self.wait(TE, self.sse, bu_guard)
                    sve_base = self.sve.n
                    om = {}
                    obanks = (0, 1) if half == 1 else (0, 1, 2, 3)
                    for oc in range(NCH):
                        b = obanks[oc % len(obanks)]
                        self.wait(TE, self.sve,
                                  om.get(oc - len(obanks), sve_base))
                        nmm = 2 * NKS + (NCH if sg == 0 else 0)
                        i = 0
                        for j in range(2):
                            for scl in range(NKS):
                                mm = em(TE, "matmul", self.bank(b),
                                        wsC[buf][:, j, scl, oc * 128:(oc + 1) * 128],
                                        Bu[:, j, scl, :],
                                        start=(i == 0), stop=(i == nmm - 1))
                                i += 1
                        if sg == 0:
                            for kc in range(NCH):
                                mm = em(TE, "matmul", self.bank(b),
                                        wsD[:, kc, oc * 128:(oc + 1) * 128],
                                        self.XN[:, kc, tau * TAU:(tau + 1) * TAU],
                                        start=False, stop=(i == nmm - 1))
                                i += 1
                        self.ste.inc(mm)
                        self.wait(VE, self.ste)
                        if sg == 0:
                            v = em(VE, "tensor_copy", oacc[:, ti, oc, :],
                                   self.bank(b))
                        else:
                            v = em(VE, "tensor_tensor", oacc[:, ti, oc, :],
                                   oacc[:, ti, oc, :], self.bank(b), ALU.add)
                        om[oc] = self.sve.inc(v)
                te_done["out"] = self.ste.n
                if sg == NSIG - 1:
                    self.wait(SY, self.ste)
                    self.wait(SY, self.sve)
                    self.wait(SY, self.g16)
                    tmk[(half, 1)] = rrds_b[1].dinc(
                        em(SY, "dma_start", xn_rt[half][:, :, :],
                           self.scr_r[2][half * NH + 1, :, :, :]))
                if blki + 2 < nblk:
                    self.wait(SY, self.ste, te_done["out"])
                    marks[blki + 2] = load_sigma(buf, (blki + 2) % NSIG)
            emit_tail_tau(1, 1)
            self.fin_full()
        self.barrier()

    def emit(self):
        with self.nc.Block():
            self.setup()
            self.aw_scope = ExitStack()
            self.aws = [self.aw_scope.enter_context(
                self.nc.sbuf_tensor("aa_ws%d" % i, [128, NCH, D], BF16))
                for i in range(2)]
            self.stage0()
            self.wait(self.SY, self.cast16, self.cast_done)
            self.aw_marks = [
                self.load_wT(self.aws[0], self.scr["Wp2"], self.wsld[0]),
                self.load_wT(self.aws[1], self.scr["Wp1"], self.wsld[1])]
            self.attn_a()
            self.aw_scope.close()
            self.ffn(1, produce_xn=True, final_out=False)
            self.attn_b()
            self.ffn(3, produce_xn=False, final_out=True)
            self.wait(self.SY, self.o16)
            self.wait(self.SY, self.o16b)
            self.wait(self.GE, self.g16)
        self.es.close()


def build_nc():
    nc = bass.Bass("TRN2", target_bir_lowering=False)
    p = Prog(nc)
    p.emit()
    build_nc.te_log = p.te_log
    return nc


def kernel(**inputs):
    x = np.ascontiguousarray(inputs["L0_attn_a_x"])
    nc = build_nc()
    in_maps = []
    for i in range(8):
        m = {k: np.ascontiguousarray(v) for k, v in inputs.items()
             if k != "L0_attn_a_x"}
        m["L0_attn_a_x"] = np.ascontiguousarray(x[i])
        in_maps.append(m)
    res = run_bass_kernel_spmd(nc, in_maps, core_ids=list(range(8)))
    return np.stack([r["y"] for r in res.results], axis=0)


# version 31
# speedup vs baseline: 1.0121x; 1.0121x over previous
import numpy as np
from contextlib import ExitStack

import concourse.bass as bass
import concourse.mybir as mybir
from concourse.bass_utils import run_bass_kernel_spmd

F32 = mybir.dt.float32
BF16 = mybir.dt.bfloat16
I32 = mybir.dt.int32
AF = mybir.ActivationFunctionType
ALU = mybir.AluOpType

D = 1024
T = 2048
NCH = 8
TAU = 512
NTAU = T // TAU
NSUB = TAU // 128
S2 = 2048
NS = 16
SIG = 512
NSIG = S2 // SIG
NKS = SIG // 128
GRP = 2
CC = 8.0
EPS = 1e-5
HALF = T // 2
NH = HALF // TAU
DEBUG = False


class Sem:
    def __init__(self, es, nc, name, owner=None):
        self.h = es.enter_context(nc.semaphore(name))
        self.n = 0
        self.owner = owner
        self.bar_n = 0

    def inc(self, instr, k=1):
        instr.then_inc(self.h, k)
        self.n += k
        return self.n

    def dinc(self, instr):
        return self.inc(instr, 16)


class Prog:
    def __init__(self, nc):
        self.nc = nc
        self.es = ExitStack()
        es = self.es
        self.sb = lambda *a: es.enter_context(nc.sbuf_tensor(*a))

        dt = lambda n, s, d, k: nc.dram_tensor(n, s, d, kind=k).ap()
        self.x_in = dt("L0_attn_a_x", [T, D], F32, "ExternalInput")
        for w in ["Wp1", "Wp2", "conv_w", "Wa", "Wx", "Wp3"]:
            setattr(self, w, dt("L0_attn_a_" + w, [D, D], F32, "ExternalInput"))
        for v in ["bp1", "bp2", "conv_b", "ba", "bx", "Lam", "bp3"]:
            setattr(self, v, dt("L0_attn_a_" + v, [D], F32, "ExternalInput"))
        for v in ["x_re", "x_im", "nu_log", "theta_log", "gamma_log"]:
            setattr(self, v, dt("L0_attn_b_" + v, [S2], F32, "ExternalInput"))
        self.B_re = dt("L0_attn_b_B_re", [S2, D], F32, "ExternalInput")
        self.B_im = dt("L0_attn_b_B_im", [S2, D], F32, "ExternalInput")
        self.C_re = dt("L0_attn_b_C_re", [D, S2], F32, "ExternalInput")
        self.C_im = dt("L0_attn_b_C_im", [D, S2], F32, "ExternalInput")
        self.Dm = dt("L0_attn_b_D", [D, D], F32, "ExternalInput")
        self.qw = dt("L0_ffn_qweights", [6, D, D], I32, "ExternalInput")
        self.qs = dt("L0_ffn_scales", [6, D, D // 16], F32, "ExternalInput")
        self.qb = dt("L0_ffn_biases", [6, D], F32, "ExternalInput")
        self.la = dt("L0_ffn_lora_a", [6, 32, D], F32, "ExternalInput")
        self.lb = dt("L0_ffn_lora_b", [6, D, 32], F32, "ExternalInput")
        self.lng = dt("L0_ffn_ln_gamma", [3, D], F32, "ExternalInput")
        self.lnb = dt("L0_ffn_ln_beta", [3, D], F32, "ExternalInput")
        self.y_out = dt("y", [T, D], F32, "ExternalOutput")

        self.scr = {}
        for w, shp in [("Wp1", [D, D]), ("Wp2", [D, D]), ("conv_w", [D, D]),
                       ("Wa", [D, D]), ("Wx", [D, D]), ("Wp3", [D, D]),
                       ("B_re", [S2, D]), ("B_im", [S2, D]),
                       ("C_re", [D, S2]), ("C_im", [D, S2]), ("Dm", [D, D])]:
            self.scr[w] = dt("scr_" + w, shp, BF16, "Internal")
        self.scr_weff = dt("scr_weff", [6, D, D], BF16, "Internal")
        self.scr_lb = dt("scr_lb", [6, D, 32], BF16, "Internal")
        self.scr_r = [dt("scr_r%d" % i, [NTAU, 128, NCH, TAU], F32, "Internal")
                      for i in range(4)]
        if DEBUG:
            self.dbg_y1 = dt("dbg_y1", [128, NCH, TAU], BF16, "Internal")
            self.dbg_y2 = dt("dbg_y2", [128, NCH, TAU], BF16, "Internal")
            self.dbg_h = dt("dbg_h", [128, NCH, HALF], F32, "Internal")
            self.dbg_hb = dt("dbg_hb", [128, NCH, HALF], BF16, "Internal")
            self.dbg_st = dt("dbg_st", [1, 2, TAU], F32, "Internal")

        sb = self.sb
        self.ident = sb("ident", [128, 128], F32)
        self.identb = sb("identb", [128, 128], BF16)
        self.ones = sb("ones", [128, 1], BF16)
        self.onesr = sb("onesr", [1, 128], F32)
        self.cv = sb("cv", [128, 352], F32)
        self.cv_off = {}
        off = 0
        for name, w in [("bp1", 8), ("bp2", 8), ("conv_b", 8), ("ba", 8),
                        ("bx", 8), ("bp3", 8), ("Lam", 8), ("splam", 8), ("nsplam", 8),
                        ("g2", 16), ("g2n", 16), ("cre2", 16), ("cim2n", 16),
                        ("x_re", 16), ("x_im", 16), ("nu_log", 16),
                        ("theta_log", 16), ("gamma_log", 16),
                        ("qb", 48), ("lng", 24), ("lnb", 24), ("tmp", 32)]:
            self.cv_off[name] = off
            off += w
        assert off <= 352
        self.stats = sb("stats", [1, 2, T], F32)
        self.stats_i = sb("stats_i", [1, 2, TAU], F32)
        self.fscr = self.stats_i[0:1, 0, :]
        self.XN = sb("XN", [128, NCH, T], BF16)
        self.ps = es.enter_context(nc.psum_tensor("ps", [128, 4096], F32))

        self.dma_sems = []

        def S(n, owner=None):
            sm = Sem(es, nc, n, owner)
            if owner is not None:
                self.dma_sems.append(sm)
            return sm
        self.g16 = S("g16", "GE")
        self.cast16 = S("cast16", "GE")
        self.cstB = S("castB", "GE")
        self.vld = S("vld", "GE")
        self.o16 = S("o16", "SY")
        self.o16b = S("o16b", "SY")
        self.cstA2 = S("castA2", "GE")
        self.wsld = [S("wsld%d" % i, "SY") for i in range(6)]
        self.ste = S("ste")
        self.sse = S("sse")
        self.sve = S("sve")
        self.sgp = S("sgp")
        self.pb = S("pb")
        self.S = S

        self.TE, self.SE, self.VE = nc.tensor, nc.scalar, nc.vector
        self.GE, self.SY = nc.gpsimd, nc.sync
        self.last = {}
        self.pend = {}
        self.statfin_ve = 0
        self.bcast_ve = 0
        self.bank_mark = {}
        self.ctx = "init"
        self.te_log = []

    def _eng(self, e):
        return {id(self.TE): "TE", id(self.SE): "SE", id(self.VE): "VE",
                id(self.GE): "GE", id(self.SY): "SY"}[id(e)]

    def em(self, e, meth, *a, **kw):
        name = self._eng(e)
        pend = self.pend.pop(name, None)
        if pend is not None:
            for sem, tgt in pend:
                if tgt > 0:
                    e.wait_ge(sem.h, tgt)
        r = getattr(e, meth)(*a, **kw)
        if name == "TE" and meth == "matmul":
            self.te_log.append((r.ins.emission_index, self.ctx))
        self.last[name] = (r, meth.startswith("dma_start"))
        return r

    def wait(self, e, sem, tgt=None):
        name = self._eng(e)
        pend = self.pend.pop(name, None)
        if pend is not None:
            for s2, t2 in pend:
                if t2 > 0:
                    e.wait_ge(s2.h, t2)
        t = tgt if tgt is not None else sem.n
        if t > 0:
            e.wait_ge(sem.h, t)

    def barrier(self, dma_sems=()):
        engs = {"TE": self.TE, "SE": self.SE, "VE": self.VE,
                "GE": self.GE, "SY": self.SY}
        for qname in ("GE", "SY"):
            owed = [sm for sm in self.dma_sems
                    if sm.owner == qname and sm.n > sm.bar_n]
            if owed or qname in self.last:
                e = engs[qname]
                for sm in owed:
                    e.wait_ge(sm.h, sm.n)
                    sm.bar_n = sm.n
                d = e.drain()
                self.pb.inc(d, 1)
                self.last.pop(qname, None)
        for name in list(self.last.keys()):
            d = getattr(engs[name], "drain")()
            self.pb.inc(d, 1)
        self.last = {}
        waits = [(self.pb, self.pb.n)]
        for name in ["TE", "SE", "VE", "GE", "SY"]:
            self.pend[name] = list(waits)

    def wait_bank(self, e, b):
        bm = self.bank_mark.get(b)
        if bm is not None:
            self.wait(e, bm[0], bm[1])

    def bank(self, b, n=512, off=0):
        return self.ps[:, 512 * b + off: 512 * b + off + n]

    def bank1(self, b, n=512):
        return self.ps[0:1, 512 * b: 512 * b + n]

    def cvc(self, name, c):
        o = self.cv_off[name] + c
        return self.cv[:, o: o + 1]

    def cvs(self, name, w):
        o = self.cv_off[name]
        return self.cv[:, o: o + w]

    def setup(self):
        GE, VE, SE = self.GE, self.VE, self.SE
        em, g16 = self.em, self.g16
        self.io_scope = ExitStack()
        self.io1 = self.io_scope.enter_context(
            self.nc.sbuf_tensor("io1", [128, 128], F32))
        self.io2 = self.io_scope.enter_context(
            self.nc.sbuf_tensor("io2", [128, 128], F32))
        em(GE, "iota", self.io1[:, :], [[0, 128]], channel_multiplier=1,
           allow_small_or_imprecise_dtypes=True)
        em(GE, "iota", self.io2[:, :], [[1, 128]], channel_multiplier=0,
           allow_small_or_imprecise_dtypes=True)
        em(GE, "memset", self.ones[:, :], 1.0 / D)
        em(GE, "memset", self.onesr[:, :], 1.0)
        em(GE, "drain")
        self.sgp.inc(em(GE, "nop"))

        def ldvec(dst_name, src, w):
            d = em(GE, "dma_start", self.cvs(dst_name, w),
                   src.rearrange("(c p) -> p c", p=128),
                   allow_slow_non_contiguous=True)
            self.vld.dinc(d)
        for n, src in [("bp1", self.bp1), ("bp2", self.bp2),
                       ("conv_b", self.conv_b), ("ba", self.ba),
                       ("bx", self.bx), ("bp3", self.bp3), ("Lam", self.Lam)]:
            ldvec(n, src, 8)
        for n, src in [("x_re", self.x_re), ("x_im", self.x_im),
                       ("nu_log", self.nu_log), ("theta_log", self.theta_log),
                       ("gamma_log", self.gamma_log)]:
            ldvec(n, src, 16)
        self.vld.dinc(em(GE, "dma_start", self.cvs("qb", 48),
                    self.qb.rearrange("k (c p) -> p (k c)", p=128),
                    allow_slow_non_contiguous=True))
        self.vld.dinc(em(GE, "dma_start", self.cvs("lng", 24),
                    self.lng.rearrange("k (c p) -> p (k c)", p=128),
                    allow_slow_non_contiguous=True))
        self.vld.dinc(em(GE, "dma_start", self.cvs("lnb", 24),
                    self.lnb.rearrange("k (c p) -> p (k c)", p=128),
                    allow_slow_non_contiguous=True))
        vec_done = self.vld.n

        self.wait(VE, self.sgp)
        em(VE, "tensor_tensor", self.ident[:, :], self.io1[:, :],
           self.io2[:, :], ALU.is_equal)
        self.ident_ve = self.sve.inc(em(VE, "drain"))
        self.wait(SE, self.sve)
        self.sse.inc(em(SE, "activation", self.identb[:, :], self.ident[:, :],
                        AF.Copy))

        cvs = self.cvs
        self.wait(SE, self.vld, vec_done)
        tmp = cvs("tmp", 16)
        t8 = tmp[:, 0:8]
        em(SE, "activation", t8, cvs("Lam", 8), AF.Exp, scale=-1.0)
        self.sse.inc(em(SE, "drain"))
        self.wait(VE, self.sse)
        em(VE, "tensor_scalar", t8, t8, 1.0, None, ALU.add)
        self.sve.inc(em(VE, "drain"))
        self.wait(SE, self.sve)
        em(SE, "activation", cvs("splam", 8), t8, AF.Ln)
        em(SE, "drain")
        em(SE, "mul", cvs("splam", 8), cvs("splam", 8), CC)
        em(SE, "drain")
        em(SE, "mul", cvs("nsplam", 8), cvs("splam", 8), -1.0)
        tmp32 = cvs("tmp", 32)
        t16 = tmp32[:, 0:16]
        t16b = tmp32[:, 16:32]
        em(SE, "drain")
        em(SE, "activation", t16, cvs("nu_log", 16), AF.Exp)
        em(SE, "drain")
        em(SE, "activation", cvs("nu_log", 16), t16, AF.Exp, scale=-1.0)
        em(SE, "drain")
        em(SE, "activation", t16b, cvs("theta_log", 16), AF.Exp)
        self.sse.inc(em(SE, "drain"))
        self.wait(VE, self.sse)
        em(VE, "tensor_scalar", t16, t16b, -np.pi / 2, None, ALU.add)
        self.sve.inc(em(VE, "drain"))
        self.wait(SE, self.sve)
        em(SE, "activation", t16, t16, AF.Sin)
        self.sse.inc(em(SE, "drain"))
        lam_mod = cvs("nu_log", 16)
        lre = cvs("g2", 16)
        self.wait(VE, self.sse)
        em(VE, "tensor_tensor", lre, lam_mod, t16, ALU.mult)
        em(VE, "drain")
        em(VE, "tensor_scalar", lre, lre, -1.0, None, ALU.mult)
        self.sve.inc(em(VE, "drain"))
        self.wait(SE, self.sve)
        em(SE, "activation", t16, t16b, AF.Sin)
        self.sse.inc(em(SE, "drain"))
        self.wait(VE, self.sse)
        lim = cvs("theta_log", 16)
        em(VE, "tensor_tensor", lim, lam_mod, t16, ALU.mult)
        em(VE, "drain")
        cre2, cim2n = cvs("cre2", 16), cvs("cim2n", 16)
        em(VE, "tensor_tensor", cre2, lre, cvs("x_re", 16), ALU.mult)
        em(VE, "tensor_tensor", t16, lim, cvs("x_im", 16), ALU.mult)
        em(VE, "drain")
        em(VE, "scalar_tensor_tensor", cre2, t16, -1.0, cre2, ALU.mult, ALU.add)
        em(VE, "drain")
        em(VE, "tensor_scalar", cre2, cre2, 2.0, None, ALU.mult)
        em(VE, "tensor_tensor", cim2n, lre, cvs("x_im", 16), ALU.mult)
        em(VE, "tensor_tensor", t16, lim, cvs("x_re", 16), ALU.mult)
        em(VE, "drain")
        em(VE, "tensor_tensor", cim2n, cim2n, t16, ALU.add)
        em(VE, "drain")
        em(VE, "tensor_scalar", cim2n, cim2n, -2.0, None, ALU.mult)
        self.sve.inc(em(VE, "drain"))
        self.wait(SE, self.sve)
        em(SE, "activation", cvs("g2", 16), cvs("gamma_log", 16), AF.Exp)
        em(SE, "drain")
        em(SE, "mul", cvs("g2", 16), cvs("g2", 16), 2.0)
        em(SE, "drain")
        em(SE, "mul", cvs("g2n", 16), cvs("g2", 16), -1.0)
        em(SE, "drain")

        for w in ["Wp2", "Wp1", "conv_w"]:
            src = getattr(self, w)
            self.cast16.dinc(em(GE, "dma_start", self.scr[w][:, :], src[:, :]))
        self.cast16.dinc(em(GE, "dma_start", self.scr_lb[:, :, :],
                            self.lb[:, :, :]))
        self.cast_done = self.cast16.n
        self.io_scope.close()

    def emit_castA2(self):
        em, GE = self.em, self.GE
        for w in ["Wa", "Wx", "Wp3"]:
            src = getattr(self, w)
            self.cstA2.dinc(em(GE, "dma_start", self.scr[w][:, :], src[:, :]))
        self.castA2_done = self.cstA2.n

    def emit_castB(self):
        em, GE = self.em, self.GE
        for w in ["B_re", "B_im", "C_re", "C_im", "Dm"]:
            src = getattr(self, w)
            self.cstB.dinc(em(GE, "dma_start", self.scr[w][:, :], src[:, :]))
        self.castB_done = self.cstB.n

    def prep_alloc(self, scope):
        nc, em = self.nc, self.em
        x2c = scope
        xi = x2c.bitcast(I32)
        xf = x2c.bitcast(F32)
        P = {}
        P["qi"] = [xi[:, 0:2, :], xi[:, 6:8, :]]
        P["waff"] = [xf[:, 0:2, :], xf[:, 6:8, :]]
        P["sc"] = [xf[:, 3, 0:128].rearrange("p (a b) -> p a b", a=2),
                   xf[:, 3, 128:256].rearrange("p (a b) -> p a b", a=2)]
        P["lab"] = x2c[0:32, 4, 0:1024]
        P["lbT"] = x2c[0:32, 5, 0:1024]
        P["lbb"] = x2c[:, 3, 1024:1280].rearrange("p (c r) -> p c r", c=NCH)
        P["qsem2"] = [self.S("p_q0", "GE"), self.S("p_q1", "GE")]
        P["qmk"] = {}
        P["aff_mark"] = {}
        P["lsem"] = self.S("p_l", "GE")
        P["wsem2"] = [self.S("p_w0", "GE"), self.S("p_w1", "GE")]
        self.weff_done = P["wsem2"]
        P["sve_use"] = {}
        P["st_mark"] = {}
        P["lbT_mark"] = 0
        P["lora_done"] = {}
        self.wait(self.GE, self.cast16, self.cast_done)
        return P

    def prep_step(self, P, step):
        self.ctx = "prep%d" % step
        """One half-weight step: loads, affine, scale, lora mms, store.
        At half==0 also loads la/lb for this k and builds lbT (TensorE
        transposes through the bank-6/7 psum region, bf16 view)."""
        em = self.em
        TE, SE, VE, GE = self.TE, self.SE, self.VE, self.GE
        k, quar = step // 4, step % 4
        buf = step % 2
        qi, waff, sc = P["qi"][buf], P["waff"][buf], P["sc"][buf]
        r0, r1 = quar * 256, quar * 256 + 256

        def emit_load(t):
            if t >= 24:
                return
            tb = t % 2
            tk, tq = t // 4, t % 4
            tr0 = tq * 256
            if t >= 2:
                self.wait(GE, self.sse, P["aff_mark"].get(t - 2, 0))
                self.wait(GE, self.sve, P["sve_use"].get(t - 2, 0))
                self.wait(GE, P["wsem2"][tb], P["st_mark"].get(t - 2, 0))
            P["qsem2"][tb].dinc(em(GE, "dma_start", P["qi"][tb][:, :, :],
                                   self.qw[tk, tr0:tr0 + 256, :].rearrange(
                                       "(c p) d -> p c d", p=128)))
            P["qmk"][t] = P["qsem2"][tb].dinc(
                em(GE, "dma_start", P["sc"][tb][:, :, :],
                   self.qs[tk, tr0:tr0 + 256, :].rearrange(
                       "(c p) f -> p c f", p=128)))

        if step == 0:
            emit_load(0)
        emit_load(step + 1)
        if quar == 0:
            self.wait(GE, self.ste, P["lora_done"].get(k - 1, 0))
            P["lsem"].dinc(em(GE, "dma_start", P["lab"][:, :],
                              self.la[k, :, :]))
            P["lsem"].dinc(em(GE, "dma_start", P["lbb"][:, :, :],
                              self.scr_lb[k, :, :].rearrange(
                                  "(c p) r -> p c r", p=128)))
            psb = self.ps.bitcast(BF16)
            self.wait(TE, P["lsem"])
            self.wait(TE, self.sse, P["lbT_mark"])
            for c in range(NCH):
                mm = em(TE, "matmul",
                        psb[0:32, 6144 + c * 128: 6144 + (c + 1) * 128],
                        P["lbb"][:, c, :], self.identb[:, :],
                        start=True, stop=True, is_transpose=True)
            self.ste.inc(mm)
            self.wait(SE, self.ste)
            P["lbT_mark"] = self.sse.inc(
                em(SE, "activation", P["lbT"][:, :], psb[0:32, 6144:7168],
                   AF.Copy))
        self.wait(SE, P["qsem2"][buf], P["qmk"][step])
        if step >= 2:
            self.wait(SE, self.sve, P["sve_use"].get(step - 2, 0))
        a = em(SE, "activation", waff[:, :, :], qi[:, :, :],
               AF.Copy, bias=-1.0, scale=1.0 / 7.5)
        P["aff_mark"][step] = self.sse.inc(a)
        scb = sc[:, :, :, None].broadcast_to((128, 2, D // 16, 16))
        self.wait(VE, self.sse)
        v = em(VE, "tensor_tensor", waff[:, :, :], waff[:, :, :],
               scb, ALU.mult)
        self.sve.inc(v)
        em(VE, "drain")
        self.wait(TE, self.sse, P["lbT_mark"])
        for idx in range(4):
            oc, dt_ = idx // 2, idx % 2
            go = 2 * quar + oc
            b = 4 + idx % 2
            self.wait_bank(TE, b)
            mm = em(TE, "matmul", self.bank(b),
                    P["lbT"][:, go * 128:(go + 1) * 128],
                    P["lab"][:, dt_ * 512:(dt_ + 1) * 512],
                    start=True, stop=True)
            self.ste.inc(mm)
            self.wait(VE, self.ste)
            v = em(VE, "tensor_tensor",
                   waff[:, oc, dt_ * 512:(dt_ + 1) * 512],
                   waff[:, oc, dt_ * 512:(dt_ + 1) * 512],
                   self.bank(b), ALU.add)
            self.bank_mark[b] = (self.sve, self.sve.inc(v))
        P["lora_done"][k] = self.ste.n
        P["sve_use"][step] = self.sve.n
        self.wait(GE, self.sve, P["sve_use"][step])
        P["st_mark"][step] = P["wsem2"][step % 2].dinc(
            em(GE, "dma_start",
               self.scr_weff[k, r0:r1, :].rearrange("(c p) d -> p c d", p=128),
               waff[:, :, :]))

    def stats_mms(self, xb_fn, xsq_fn, tau_w=TAU):
        em, TE = self.em, self.TE
        self.wait(TE, self.sve, self.statfin_ve)
        self.wait(TE, self.sse)
        for c in range(NCH):
            em(TE, "matmul", self.bank1(3, tau_w), self.ones[:, :],
               xb_fn(c), start=(c == 0), stop=(c == NCH - 1))
        for c in range(NCH):
            mm = em(TE, "matmul", self.bank1(2, tau_w), self.ones[:, :],
                    xsq_fn(c), start=(c == 0), stop=(c == NCH - 1))
        self.ste.inc(mm)
        return self.ste.n

    def stats_copy(self, tau, tau_w=TAU):
        em, VE = self.em, self.VE
        self.wait(VE, self.ste)
        em(VE, "tensor_copy", self.stats[0:1, 0, tau * tau_w:(tau + 1) * tau_w],
           self.bank1(3, tau_w))
        v = em(VE, "tensor_copy",
               self.stats[0:1, 1, tau * tau_w:(tau + 1) * tau_w],
               self.bank1(2, tau_w))
        self.statfin_ve = self.sve.inc(v)
        return self.statfin_ve

    def fin_full(self, ntau=NTAU):
        em, VE, SE = self.em, self.VE, self.SE
        em(VE, "drain")
        for tau in range(ntau):
            m_ap = self.stats[0:1, 0, tau * TAU:(tau + 1) * TAU]
            rs_ap = self.stats[0:1, 1, tau * TAU:(tau + 1) * TAU]
            em(VE, "tensor_tensor", self.fscr[:, :], m_ap, m_ap, ALU.mult)
            em(VE, "drain")
            em(VE, "scalar_tensor_tensor", rs_ap, self.fscr[:, :], -1.0,
               rs_ap, ALU.mult, ALU.add)
            em(VE, "drain")
        rs_full = self.stats[0:1, 1, 0:ntau * TAU]
        em(VE, "tensor_scalar", rs_full, rs_full, EPS, None, ALU.add)
        self.sve.inc(em(VE, "drain"))
        self.wait(SE, self.sve)
        em(SE, "activation", rs_full, rs_full, AF.Ln)
        em(SE, "drain")
        self.sse.inc(em(SE, "activation", rs_full, rs_full, AF.Exp, scale=-0.5))
        return self.sse.n

    def stats_fin(self, dst_m, dst_rs, tau_w=TAU):
        em, VE, SE = self.em, self.VE, self.SE
        self.wait(VE, self.ste)
        em(VE, "tensor_copy", dst_m, self.bank1(3, tau_w))
        v = em(VE, "tensor_scalar", dst_rs, self.bank1(2, tau_w), EPS, None,
               ALU.add)
        self.statfin_ve = self.sve.inc(v)
        em(VE, "drain")
        em(VE, "tensor_tensor", self.fscr[:, 0:tau_w], dst_m, dst_m,
           ALU.mult)
        em(VE, "drain")
        em(VE, "scalar_tensor_tensor", dst_rs, self.fscr[:, 0:tau_w], -1.0,
           dst_rs, ALU.mult, ALU.add)
        self.sve.inc(em(VE, "drain"))
        self.wait(SE, self.sve)
        em(SE, "activation", dst_rs, dst_rs, AF.Ln)
        em(SE, "drain")
        self.sse.inc(em(SE, "activation", dst_rs, dst_rs, AF.Exp, scale=-0.5))
        return self.sse.n

    def bcast(self, m_ap, rs_ap, tau_w=TAU):
        em, TE = self.em, self.TE
        self.wait(TE, self.sse)
        self.wait(TE, self.sve, self.bcast_ve)
        em(TE, "matmul", self.bank(6, tau_w), self.onesr[:, :], m_ap,
           start=True, stop=True)
        mm = em(TE, "matmul", self.bank(7, tau_w), self.onesr[:, :], rs_ap,
                start=True, stop=True)
        self.ste.inc(mm)
        return self.ste.n

    def ln_apply(self, src_f32, dst_bf16, tmp_f32):
        em, VE = self.em, self.VE
        self.wait(VE, self.ste)
        for c in range(NCH):
            em(VE, "scalar_tensor_tensor", tmp_f32[:, c, :], self.bank(6),
               -1.0, src_f32[:, c, :], ALU.mult, ALU.add)
        em(VE, "drain")
        for c in range(NCH):
            v = em(VE, "tensor_tensor", dst_bf16[:, c, :], tmp_f32[:, c, :],
                   self.bank(7), ALU.mult)
        self.bcast_ve = self.sve.inc(v)
        return self.sve.n

    def load_wT(self, slot_ap, scr_ap, sem, kchunks=NCH):
        em, SY = self.em, self.SY
        for kc in range(kchunks):
            sem.dinc(em(SY, "dma_start_transpose", slot_ap[:, kc, :],
                        scr_ap[:, kc * 128:(kc + 1) * 128]))
        return sem.n

    def proj(self, wslot, src, tau, evac, tau_w=TAU, src_tau=None,
             banks=(0, 1)):
        em, TE, SE = self.em, self.TE, self.SE
        if src_tau is None:
            src_tau = tau
        for oc in range(NCH):
            b = banks[oc % len(banks)]
            self.wait_bank(TE, b)
            for kc in range(NCH):
                mm = em(TE, "matmul", self.bank(b, tau_w),
                        wslot[:, kc, oc * 128:(oc + 1) * 128],
                        src[:, kc, src_tau * tau_w:(src_tau + 1) * tau_w],
                        start=(kc == 0), stop=(kc == NCH - 1))
            self.ste.inc(mm)
            self.wait(SE, self.ste)
            s = evac(oc, self.bank(b, tau_w))
            self.bank_mark[b] = (self.sse, self.sse.inc(s))
        return self.sse.n

    def stage0(self):
        nc, em = self.nc, self.em
        self.ctx = "stage0"
        TE, SE, VE, GE, SY = self.TE, self.SE, self.VE, self.GE, self.SY
        with ExitStack() as ph:
            sb = lambda *a: ph.enter_context(nc.sbuf_tensor(*a))
            xt = sb("s0_xt", [128, 2, D], F32)
            r0 = sb("s0_r0", [128, NCH, T], F32)
            tmpf = sb("s0_tmp", [128, NCH, TAU], F32)
            rb = sb("s0_rb", [128, NCH, TAU], BF16)
            rsq = sb("s0_rsq", [128, NCH, TAU], BF16)
            xsem = [self.S("s0x0", "SY"), self.S("s0x1", "SY")]
            ld_marks = []
            self.wait(SY, self.sgp)
            self.wait(TE, self.sve, self.ident_ve)
            for st in range(2):
                xsem[st % 2].dinc(em(SY, "dma_start", xt[:, st, :],
                                     self.x_in[st * 128:(st + 1) * 128, :]))
                ld_marks.append(xsem[st % 2].n)
            evac_mark = {}
            for tau in range(NTAU):
                for st in range(NSUB):
                    gst = tau * NSUB + st
                    buf = gst % 2
                    bb = (6, 4, 0)[gst % 3]
                    self.wait(TE, xsem[gst % 2], ld_marks[gst])
                    if gst >= 3:
                        self.wait(TE, self.sse, evac_mark[gst - 3])
                    if bb == 6:
                        self.wait(TE, self.sve, self.bcast_ve)
                    for c in range(NCH):
                        mm = em(TE, "matmul",
                                self.ps[:, 512 * bb + 128 * c:
                                        512 * bb + 128 * (c + 1)],
                                xt[:, buf, 128 * c:128 * (c + 1)],
                                self.ident[:, :], start=True, stop=True,
                                is_transpose=True)
                    te_mark = self.ste.inc(mm)
                    if gst + 2 < T // 128:
                        self.wait(SY, self.ste, te_mark)
                        xsem[buf].dinc(em(SY, "dma_start", xt[:, buf, :],
                                          self.x_in[(gst + 2) * 128:(gst + 3) * 128, :]))
                        ld_marks.append(xsem[buf].n)
                    self.wait(SE, self.ste, te_mark)
                    src = self.ps[:, 512 * bb: 512 * bb + 1024].rearrange(
                        "p (c f) -> p c f", f=128)
                    off = tau * TAU + 128 * st
                    em(SE, "copy", r0[:, :, off:off + 128], src)
                    em(SE, "activation", rb[:, :, 128 * st:128 * (st + 1)],
                       src, AF.Copy)
                    sI = em(SE, "activation", rsq[:, :, 128 * st:128 * (st + 1)],
                            src, AF.Square)
                    evac_mark[gst] = self.sse.inc(sI)
                self.stats_mms(lambda c: rb[:, c, :], lambda c: rsq[:, c, :])
                m_ap = self.stats[0:1, 0, tau * TAU:(tau + 1) * TAU]
                rs_ap = self.stats[0:1, 1, tau * TAU:(tau + 1) * TAU]
                self.stats_fin(m_ap, rs_ap)
                self.wait(SE, self.ste)
                self.wait(GE, self.sse, evac_mark[tau * NSUB + NSUB - 1])
                self.g16.dinc(em(GE, "dma_start",
                                 self.scr_r[0][tau, :, :, :],
                                 r0[:, :, tau * TAU:(tau + 1) * TAU]))
                self.bcast(m_ap, rs_ap)
                self.ln_apply(r0[:, :, tau * TAU:(tau + 1) * TAU],
                              self.XN[:, :, tau * TAU:(tau + 1) * TAU], tmpf)
        self.barrier()

    def attn_a(self):
        nc, em = self.nc, self.em
        self.ctx = "attn_a"
        TE, SE, VE, GE, SY = self.TE, self.SE, self.VE, self.GE, self.SY
        with ExitStack() as ph:
            sb = lambda *a: ph.enter_context(nc.sbuf_tensor(*a))
            x1b = sb("aa_x1b", [128, NCH, T], BF16)
            ws = self.aws
            rrd = self.S("aa_rrd", "SY")
            wp3s = sb("aa_wp3", [128, NCH, D], BF16)
            m_wp2, m_wp1 = self.aw_marks

            ph1 = ExitStack()
            x2c = ph1.enter_context(nc.sbuf_tensor("aa_x2c", [128, NCH, T], BF16))
            m_conv = self.load_wT(wp3s, self.scr["conv_w"], self.wsld[2])
            self.emit_castA2()
            phA = ExitStack()
            x2b = phA.enter_context(nc.sbuf_tensor("aa_x2b", [128, NCH, T], BF16))
            with ExitStack() as phP:
                P = self.prep_alloc(x2c)
                nstep = [0]

                def drive_prep(slots_done):
                    want = min(24, (slots_done * 24 + 7) // 8)
                    while nstep[0] < want:
                        self.prep_step(P, nstep[0])
                        nstep[0] += 1

                slot = [0]
                self.wait(TE, self.wsld[0], m_wp2)
                for tau in range(NTAU):
                    self.proj(ws[0], self.XN, tau,
                              lambda oc, pb, _t=tau: em(
                                  SE, "activation",
                                  x2b[:, oc, _t * TAU:(_t + 1) * TAU], pb,
                                  AF.Identity, bias=self.cvc("bp2", oc)))
                    slot[0] += 1
                    drive_prep(slot[0])
                self.wait(TE, self.wsld[1], m_wp1)
                for tau in range(NTAU):
                    self.proj(ws[1], self.XN, tau,
                              lambda oc, pb, _t=tau: em(
                                  SE, "activation",
                                  x1b[:, oc, _t * TAU:(_t + 1) * TAU], pb,
                                  AF.Gelu, bias=self.cvc("bp1", oc)))
                    slot[0] += 1
                    drive_prep(slot[0])
            if True:
                self.wait(TE, self.wsld[2], m_conv)
                self.wait(SY, self.ste)
                self.wait(SY, self.cstA2, self.castA2_done)
                m_wa = self.load_wT(ws[0], self.scr["Wa"], self.wsld[0])
                m_wx = self.load_wT(ws[1], self.scr["Wx"], self.wsld[1])
                self.wait(SE, self.sve, P["sve_use"][23])
                for _w in P["wsem2"]:
                    self.wait(SE, _w)
                for tau in range(NTAU):
                    self.proj(wp3s, x2b, tau,
                              lambda oc, pb, _t=tau: em(
                                  SE, "activation",
                                  x2c[:, oc, _t * TAU:(_t + 1) * TAU], pb,
                                  AF.Relu, bias=self.cvc("conv_b", oc)),
                              banks=(0, 1, 4, 5))
                conv_te_done = self.ste.n
                phA.close()

            with ExitStack() as phB:
                NB = 3
                gsc = phB.enter_context(
                    nc.sbuf_tensor("aa_gsc", [128, NB, T], F32))
                tf = phB.enter_context(
                    nc.sbuf_tensor("aa_tf", [128, T], F32))
                rg = phB.enter_context(
                    nc.sbuf_tensor("aa_rg", [128, T], BF16))
                gscC = wp3s.rearrange("p c d -> p (c d)").rearrange(
                    "p (a b) -> p a b", a=4)[:, 0:NB, :]
                self.wait(TE, self.wsld[0], m_wa)
                self.wait(TE, self.wsld[1], m_wx)
                prod_done = 0
                aexp_mark = {}

                def emit_gates(c):
                    self.ctx = "a3.gates%d" % c
                    """TE gate matmuls + SE sigmoid evacs + a = exp."""
                    bufc = c % NB
                    em(SE, "drain")
                    for kind, wslot, coff, b0, dstT in (
                            (0, ws[0], "ba", 0, None),
                            (1, ws[1], "bx", 2, gscC)):
                        for ft in range(NTAU):
                            b = ((0, 1, 4, 5) if kind == 0
                                 else (2, 3, 6, 7))[ft % 4]
                            self.wait_bank(TE, b)
                            for kc in range(NCH):
                                mm = em(TE, "matmul", self.bank(b),
                                        wslot[:, kc, c * 128:(c + 1) * 128],
                                        x2c[:, kc, ft * TAU:(ft + 1) * TAU],
                                        start=(kc == 0), stop=(kc == NCH - 1))
                            self.ste.inc(mm)
                            self.wait(SE, self.ste)
                            if kind == 1 and ft == 0 and c >= NB:
                                self.wait(SE, self.sve, self._p2done.get(c - NB, 0))
                            if kind == 0:
                                dst = rg[:, ft * TAU:(ft + 1) * TAU]
                            else:
                                dst = dstT[:, bufc, ft * TAU:(ft + 1) * TAU]
                            sI = em(SE, "activation", dst, self.bank(b),
                                    AF.Sigmoid, bias=self.cvc(coff, c))
                            self.bank_mark[b] = (self.sse, self.sse.inc(sI))
                    em(SE, "drain")
                    if c >= NB:
                        self.wait(SE, self.sve, self._p2done.get(c - NB, 0))
                    sI = em(SE, "activation", gsc[:, bufc, :], rg[:, :],
                            AF.Exp, scale=self.cvc("nsplam", c))
                    aexp_mark[c] = self.sse.inc(sI)

                def emit_chain(c):
                    self.ctx = "a3.chain%d" % c
                    """t = clamp(1-a^2); b = exp(.5 ln t); g = ig*x2c*b;
                    ys = scan(a, g); prod *= ys.  Tf is the fp32 work slot."""
                    bufc = c % NB
                    A = gsc[:, bufc, :]
                    C = gscC[:, bufc, :]
                    self.wait(VE, self.sse, aexp_mark[c])
                    em(VE, "drain")
                    em(VE, "tensor_tensor", tf[:, :], A, A, ALU.mult)
                    em(VE, "drain")
                    em(VE, "tensor_scalar", tf[:, :], tf[:, :], -1.0, 1.0,
                       ALU.mult, ALU.add)
                    em(VE, "drain")
                    em(VE, "tensor_scalar", tf[:, :], tf[:, :], 1e-30, None,
                       ALU.max)
                    em(VE, "drain")
                    v = em(VE, "tensor_tensor", C, C, x2c[:, c, :], ALU.mult)
                    vm = self.sve.inc(v)
                    self.wait(SE, self.sve, vm)
                    em(SE, "drain")
                    em(SE, "activation", tf[:, :], tf[:, :], AF.Ln)
                    em(SE, "drain")
                    sI = em(SE, "activation", tf[:, :], tf[:, :], AF.Exp,
                            scale=0.5)
                    sm = self.sse.inc(sI)
                    self.wait(VE, self.sse, sm)
                    em(VE, "tensor_tensor", C, C, tf[:, :], ALU.mult)
                    em(VE, "drain")
                    em(VE, "tensor_tensor_scan", tf[:, :], A, C, 0.0,
                       ALU.mult, ALU.add)
                    em(VE, "drain")
                    v = em(VE, "tensor_tensor", x1b[:, c, :], x1b[:, c, :],
                           tf[:, :], ALU.mult)
                    self._p2done[c] = self.sve.inc(v)
                    return self._p2done[c]

                self._p2done = {}
                for c in range(NCH):
                    emit_gates(c)
                    if c >= 2:
                        emit_chain(c - 2)
                prod_done6 = emit_chain(NCH - 2)
                prod_done = emit_chain(NCH - 1)
            self.wait(SY, self.ste)
            self.wait(SY, self.cstA2, self.castA2_done)
            m_wp3 = self.load_wT(ws[0], self.scr["Wp3"], self.wsld[2])
            self.emit_castB()
            ph1.close()
            prod_marks = dict(self._p2done)

            with ExitStack() as phC:
                sbC = lambda *a: phC.enter_context(nc.sbuf_tensor(*a))
                rtile = sbC("aa_rt", [128, NCH, TAU], F32)
                otile = sbC("aa_ot", [128, 2, NCH, TAU], F32)
                rb = sbC("aa_rb", [128, NCH, TAU], BF16)
                rsq = sbC("aa_rsq", [128, NCH, TAU], BF16)
                self.wait(TE, self.wsld[2], m_wp3)
                self.wait(SY, self.sve, prod_done)
                r1_done = {}
                spsem = [self.S("aa_sp0", "GE"), self.S("aa_sp1", "GE")]
                spmark = {}

                def a4_stats(tau):
                    self.ctx = "a4.stats%d" % tau
                    pp = tau % 2
                    self.wait(SE, self.sve, r1_done[tau])
                    self.wait(SE, self.ste)
                    sI = em(SE, "activation", rb[:, :, :],
                            otile[:, pp, :, :], AF.Copy)
                    self.sse.inc(sI)
                    em(VE, "drain")
                    v = em(VE, "tensor_tensor", rsq[:, :, :],
                           otile[:, pp, :, :], otile[:, pp, :, :], ALU.mult)
                    self.sve.inc(v)
                    self.wait(TE, self.sve)
                    self.stats_mms(lambda c: rb[:, c, :],
                                   lambda c: rsq[:, c, :])
                    self.stats_fin(self.stats[0:1, 0, tau * TAU:(tau + 1) * TAU],
                                   self.stats[0:1, 1, tau * TAU:(tau + 1) * TAU])
                    self.wait(GE, self.sve, r1_done[tau])
                    spmark[tau] = spsem[pp].dinc(
                        em(GE, "dma_start", self.scr_r[1][tau, :, :, :],
                           otile[:, pp, :, :]))

                def proj_split(tau, pp):
                    self.ctx = "a4.proj%d" % tau
                    """Wp3 projection with k 0..5 gated only on chain(5),
                    k 6..7 gated on the final chains -- lets A4 start while
                    the last two gate chains still run on VE/SE."""
                    for oc in range(NCH):
                        b = (0, 1, 4, 5)[oc % 4]
                        self.wait_bank(TE, b)
                        if oc == 0:
                            self.wait(TE, self.sve, prod_marks[5])
                        for kc in range(6):
                            em(TE, "matmul", self.bank(b),
                               ws[0][:, kc, oc * 128:(oc + 1) * 128],
                               x1b[:, kc, tau * TAU:(tau + 1) * TAU],
                               start=(kc == 0), stop=False)
                        if oc == 0:
                            self.wait(TE, self.sve, prod_marks[7])
                        for kc in (6, 7):
                            mm = em(TE, "matmul", self.bank(b),
                                    ws[0][:, kc, oc * 128:(oc + 1) * 128],
                                    x1b[:, kc, tau * TAU:(tau + 1) * TAU],
                                    start=False, stop=(kc == 7))
                        self.ste.inc(mm)
                        self.wait(SE, self.ste)
                        sI = em(SE, "activation", otile[:, pp, oc, :],
                                self.bank(b), AF.Identity,
                                bias=self.cvc("bp3", oc))
                        self.bank_mark[b] = (self.sse, self.sse.inc(sI))

                for tau in range(NTAU):
                    pp = tau % 2
                    if tau >= 1:
                        self.wait(SY, self.sve)
                    rrd.dinc(em(SY, "dma_start", rtile[:, :, :],
                                self.scr_r[0][tau, :, :, :]))
                    if tau >= 2:
                        self.wait(SE, spsem[tau % 2], spmark[tau - 2])
                    if tau <= 1:
                        proj_split(tau, pp)
                    else:
                        self.proj(ws[0], x1b, tau,
                                  lambda oc, pb, _pp=pp: em(SE, "activation",
                                                    otile[:, _pp, oc, :], pb,
                                                    AF.Identity,
                                                    bias=self.cvc("bp3", oc)),
                                  banks=(0, 1, 4, 5))
                    self.wait(VE, rrd)
                    self.wait(VE, self.sse)
                    for c in range(NCH):
                        v = em(VE, "tensor_tensor", otile[:, pp, c, :],
                               otile[:, pp, c, :], rtile[:, c, :], ALU.add)
                    r1_done[tau] = self.sve.inc(v)
                    if tau >= 1:
                        a4_stats(tau - 1)
                a4_stats(NTAU - 1)
        self.barrier()

    def ffn(self, ri, produce_xn, final_out):
        self.fctx = "ffn%d" % ri
        """FFN instance with ti-level depth-1 software pipeline within each
        half: TE runs q1(t0), q1(t1)+stats(t0), bcast(t0), q2(t0)+stats(t1),
        bcast(t1), q2(t1) while the LN chains run on VE/SE in the gaps.
        q2 output is added to h directly from PSUM on VE (with per-partition
        bias via scalar_tensor_tensor)."""
        nc, em = self.nc, self.em
        TE, SE, VE, GE, SY = self.TE, self.SE, self.VE, self.GE, self.SY
        with ExitStack() as ph:
            sb = lambda n, *a: ph.enter_context(
                nc.sbuf_tensor(n + "_i%d" % ri, *a))
            H = sb("f_h", [128, NCH, HALF], F32)
            rtile = sb("f_rt", [128, NCH, TAU], F32)
            ws = [sb("f_ws%d" % i, [128, NCH, D], BF16) for i in range(3)]
            y1b = sb("f_y1b", [128, 2, NCH, TAU], BF16)
            y2b = sb("f_y2b", [128, 2, NCH, TAU], BF16)
            tmpb = sb("f_tmpb", [128, NCH, TAU], BF16)
            otm = None
            if not final_out:
                rb = y1b[:, 0]
            rrd = self.S("f_rrd%d" % ri, "SY")
            rrds = [rrd, self.S("f_rrd2%d" % ri, "SY")]
            rt2 = y2b.bitcast(F32).rearrange("p a c t -> p (a c t)")
            rt2 = rt2.rearrange("p (c t) -> p c t", c=NCH)
            rts = [rtile, rt2]

            def q1(ti, k0, wslot, hb, ocs=range(NCH)):
                self.ctx = self.fctx + ".q1(%d)" % ti
                """TE q1-mms for the given ocs; SE evac -> y1b[ti%2] (+bias)."""
                p = ti % 2
                for oc in ocs:
                    b = (0, 1, 4, 5)[oc % 4]
                    self.wait_bank(TE, b)
                    for kc in range(NCH):
                        mm = em(TE, "matmul", self.bank(b),
                                wslot[:, kc, oc * 128:(oc + 1) * 128],
                                hb[:, kc, ti * TAU:(ti + 1) * TAU],
                                start=(kc == 0), stop=(kc == NCH - 1))
                    self.ste.inc(mm)
                    self.wait(VE, self.ste)
                    v = em(VE, "tensor_scalar", y1b[:, p, oc, :], self.bank(b),
                           self.cvc("qb", k0 * 8 + oc), None, ALU.add)
                    self.bank_mark[b] = (self.sve, self.sve.inc(v))

            sqf = sb("f_sqf", [128, NCH, TAU], BF16)

            def q1_sq_se(ti):
                p = ti % 2
                self.wait(VE, self.ste)
                em(VE, "drain")
                v = em(VE, "tensor_tensor", sqf[:, :, :], y1b[:, p, :, :],
                       y1b[:, p, :, :], ALU.mult)
                self.sve.inc(v)

            def q1_sq_te(ti):
                self.ctx = self.fctx + ".stats(%d)" % ti
                """TE: stats mms block over y1b + sqf."""
                p = ti % 2
                self.wait(TE, self.sve, self.statfin_ve)
                self.wait(TE, self.sve)
                for oc in range(NCH):
                    em(TE, "matmul", self.bank1(3), self.ones[:, :],
                       y1b[:, p, oc, :], start=(oc == 0), stop=(oc == NCH - 1))
                for oc in range(NCH):
                    mm = em(TE, "matmul", self.bank1(2), self.ones[:, :],
                            sqf[:, oc, :], start=(oc == 0),
                            stop=(oc == NCH - 1))
                self.ste.inc(mm)

            sti_t = [self.stats_i, None]
            mrs = sb("f_mrs", [128, 2, 2, TAU], BF16)

            def stats_tail(ti):
                p = ti % 2
                self.wait(VE, self.ste)
                em(VE, "tensor_copy", sti_t[p][0:1, 0, :], self.bank1(3))
                v = em(VE, "tensor_scalar", sti_t[p][0:1, 1, :], self.bank1(2),
                       EPS, None, ALU.add)
                self.statfin_ve = self.sve.inc(v)

            def fin_i(ti):
                p = ti % 2
                m_ap = sti_t[p][0:1, 0, :]
                rs_ap = sti_t[p][0:1, 1, :]
                scr = sti_t[1 - p][0:1, 0, :]
                self.wait(VE, self.ste)
                em(VE, "drain")
                em(VE, "tensor_tensor", scr, m_ap, m_ap, ALU.mult)
                em(VE, "drain")
                em(VE, "scalar_tensor_tensor", rs_ap, scr, -1.0,
                   rs_ap, ALU.mult, ALU.add)
                self.sve.inc(em(VE, "drain"))
                self.wait(SE, self.sve)
                em(SE, "activation", rs_ap, rs_ap, AF.Ln)
                em(SE, "drain")
                sI = em(SE, "activation", rs_ap, rs_ap, AF.Exp, scale=-0.5)
                return m_ap, rs_ap, self.sse.inc(sI)

            def bcast_evac(ti, bmk=None):
                p = ti % 2
                self.wait(VE, self.ste, bmk)
                em(VE, "tensor_copy", mrs[:, 0, p, :], self.bank(6))
                v = em(VE, "tensor_copy", mrs[:, 1, p, :], self.bank(7))
                self.bcast_ve = self.sve.inc(v)

            def apply_gelu(ti, sbk):
                p = ti % 2
                if final_out:
                    self.wait(VE, self.o16)
                em(VE, "drain")
                for c in range(NCH):
                    em(VE, "tensor_tensor", tmpb[:, c, :], y1b[:, p, c, :],
                       mrs[:, 0, p, :], ALU.subtract)
                em(VE, "drain")
                for c in range(NCH):
                    v = em(VE, "tensor_tensor", tmpb[:, c, :],
                           tmpb[:, c, :], mrs[:, 1, p, :], ALU.mult)
                self.sve.inc(v)
                self.wait(SE, self.sve)
                for c in range(NCH):
                    sI = em(SE, "activation", y2b[:, p, c, :],
                            tmpb[:, c, :], AF.Gelu,
                            scale=self.cvc("lng", sbk * 8 + c),
                            bias=self.cvc("lnb", sbk * 8 + c))
                self.sse.inc(sI)

            def q2_hadd(ti, k1, wslot, ocs=range(NCH)):
                self.ctx = self.fctx + ".q2(%d)" % ti
                """TE q2-mms; VE adds psum (+bias) into H directly."""
                p = ti % 2
                hs = H[:, :, ti * TAU:(ti + 1) * TAU]
                for oc in ocs:
                    b = (0, 1, 4, 5)[oc % 4]
                    self.wait_bank(TE, b)
                    for kc in range(NCH):
                        mm = em(TE, "matmul", self.bank(b),
                                wslot[:, kc, oc * 128:(oc + 1) * 128],
                                y2b[:, p, kc, :],
                                start=(kc == 0), stop=(kc == NCH - 1))
                    self.ste.inc(mm)
                    self.wait(VE, self.ste)
                    v = em(VE, "scalar_tensor_tensor", hs[:, oc, :],
                           self.bank(b), self.cvc("qb", k1 * 8 + oc),
                           hs[:, oc, :], ALU.add, ALU.add)
                    self.bank_mark[b] = (self.sve, self.sve.inc(v))
                return self.sve.n

            for _w in self.weff_done:
                self.wait(SY, _w)
            wload_mark = {}
            wuse_mark = {}
            nextk = [0]

            def load_next_weff(slot):
                k = nextk[0]
                if k >= 12:
                    return
                self.wait(SY, self.ste, wuse_mark.get(slot, 0))
                m = self.load_wT(ws[slot], self.scr_weff[k % 6, :, :],
                                 self.wsld[3 + slot])
                wload_mark[k] = (slot, m)
                nextk[0] += 1

            self.wait(SY, self.sve)
            self.wait(SY, self.sse)
            self.wait(SY, self.ste)
            self.wait(SY, self.g16)
            hmk0 = {}
            for ti in range(NH):
                hmk0[ti] = rrds[ti % 2].dinc(
                    em(SY, "dma_start", rts[ti % 2][:, :, :],
                       self.scr_r[ri][ti, :, :, :]))
            load_next_weff(0)
            load_next_weff(1)
            load_next_weff(2)
            omk = {0: 0, 1: 0}
            for half in range(2):
                t0 = half * NH
                self.fctx = "ffn%d.h%d" % (ri, half)
                self.ctx = self.fctx + ".hinit"
                sti_t[1] = self.stats[0:1, :, (t0 + 1) * TAU:(t0 + 2) * TAU]
                hb = self.XN[:, :, half * HALF:(half + 1) * HALF]
                hb_mk = {}
                if half == 0:
                    hmk = hmk0
                else:
                    self.wait(SY, self.sve)
                    self.wait(SY, self.sse)
                    self.wait(SY, self.ste)
                    self.wait(SY, self.g16)
                    hmk = {}
                    for ti in range(NH):
                        hmk[ti] = rrds[ti % 2].dinc(
                            em(SY, "dma_start", rts[ti % 2][:, :, :],
                               self.scr_r[ri][(t0 + ti), :, :, :]))
                for ti in range(NH):
                    tau = t0 + ti
                    rtl = rts[ti % 2]
                    m_ap = self.stats[0:1, 0, tau * TAU:(tau + 1) * TAU]
                    rs_ap = self.stats[0:1, 1, tau * TAU:(tau + 1) * TAU]
                    self.bcast(m_ap, rs_ap)
                    self.wait(VE, rrds[ti % 2], hmk[ti])
                    hs = H[:, :, ti * TAU:(ti + 1) * TAU]
                    self.wait(VE, self.ste)
                    for c in range(NCH):
                        em(VE, "scalar_tensor_tensor", hs[:, c, :], self.bank(6),
                           -1.0, rtl[:, c, :], ALU.mult, ALU.add)
                    em(VE, "drain")
                    for c in range(NCH):
                        v = em(VE, "tensor_tensor", hs[:, c, :], hs[:, c, :],
                               self.bank(7), ALU.mult)
                    self.bcast_ve = self.sve.inc(v)
                    em(VE, "drain")
                    v = em(VE, "tensor_copy", hb[:, :, ti * TAU:(ti + 1) * TAU],
                           hs)
                    hb_mk[ti] = self.sve.inc(v)
                for sbk in range(3):
                    self.fctx = self.fctx.split(".sb")[0] + ".sb%d" % sbk
                    self.ctx = self.fctx
                    k0, k1 = 2 * sbk, 2 * sbk + 1
                    kk = half * 6 + 2 * sbk
                    s0, m0l = wload_mark[kk]
                    s1, m1l = wload_mark[kk + 1]
                    w0slot, w1slot = ws[s0], ws[s1]
                    self.wait(TE, self.wsld[3 + s0], m0l)
                    self.wait(TE, self.sve, hb_mk[0])
                    if final_out and sbk == 0:
                        self.wait(VE, self.o16b)
                    q1(0, k0, w0slot, hb)
                    q1_sq_se(0)
                    self.wait(TE, self.sve, hb_mk[1])
                    q1(1, k0, w0slot, hb, range(0, 4))
                    q1_sq_te(0)
                    stats_tail(0)
                    m0, rs0, f0 = fin_i(0)
                    q1(1, k0, w0slot, hb, range(4, 8))
                    wuse_mark[s0] = self.ste.n
                    load_next_weff(s0)
                    q1_sq_se(1)
                    self.wait(TE, self.sse, f0)
                    self.bcast(m0, rs0)
                    bcast_evac(0)
                    q1_sq_te(1)
                    stats_tail(1)
                    apply_gelu(0, sbk)
                    g0 = self.sse.n
                    m1, rs1, f1 = fin_i(1)
                    self.wait(TE, self.sse, f1)
                    bmk = self.bcast(m1, rs1)
                    bcast_evac(1, bmk)
                    apply_gelu(1, sbk)
                    g1 = self.sse.n
                    self.wait(TE, self.wsld[3 + s1], m1l)
                    self.wait(TE, self.sse, g0)
                    q2_hadd(0, k1, w1slot)
                    if sbk == 2:
                        self.wait(SY, self.sve)
                        self.wait(SY, self.sse)
                        self.wait(SY, self.g16)
                        tmk0 = rrds[0].dinc(
                            em(SY, "dma_start", rtile[:, :, :],
                               self.scr_r[ri][t0, :, :, :]))
                    if sbk < 2:
                        em(VE, "drain")
                        v = em(VE, "tensor_copy", hb[:, :, 0:TAU],
                               H[:, :, 0:TAU])
                        hb_mk[0] = self.sve.inc(v)
                    self.wait(TE, self.sse, g1)
                    h1_done = q2_hadd(1, k1, w1slot)
                    wuse_mark[s1] = self.ste.n
                    load_next_weff(s1)
                    if sbk < 2:
                        em(VE, "drain")
                        v = em(VE, "tensor_copy", hb[:, :, TAU:HALF],
                               H[:, :, TAU:HALF])
                        hb_mk[1] = self.sve.inc(v)
                self.ctx = "ffn%d.h%d.tail" % (ri, half)
                self.wait(SY, self.ste)
                self.wait(SY, self.sse)
                self.wait(SY, self.g16)
                tmk = {0: tmk0,
                       1: rrds[1].dinc(em(SY, "dma_start", rt2[:, :, :],
                                          self.scr_r[ri][t0 + 1, :, :, :]))}
                em(VE, "drain")
                for ti in range(NH):
                    tau = t0 + ti
                    rtl = rts[ti % 2]
                    self.wait(VE, rrds[ti % 2], tmk[ti])
                    hs = H[:, :, ti * TAU:(ti + 1) * TAU]
                    for c in range(NCH):
                        v = em(VE, "tensor_tensor", rtl[:, c, :], hs[:, c, :],
                               rtl[:, c, :], ALU.add)
                    rdone = self.sve.inc(v)
                    if not final_out:
                        self.wait(SE, self.sve, rdone)
                        self.wait(SE, self.ste)
                        sI = em(SE, "activation", rb[:, :, :], rtl[:, :, :],
                                AF.Copy)
                        self.sse.inc(sI)
                        em(VE, "drain")
                        v = em(VE, "tensor_tensor", sqf[:, :, :],
                               rtl[:, :, :], rtl[:, :, :], ALU.mult)
                        self.sve.inc(v)
                        self.wait(TE, self.sve)
                        self.stats_mms(lambda c: rb[:, c, :],
                                       lambda c: sqf[:, c, :])
                        m_ap = self.stats[0:1, 0, tau * TAU:(tau + 1) * TAU]
                        rs_ap = self.stats[0:1, 1, tau * TAU:(tau + 1) * TAU]
                        self.stats_fin(m_ap, rs_ap)
                        self.wait(GE, self.sve, rdone)
                        self.g16.dinc(em(GE, "dma_start",
                                         self.scr_r[ri + 1][tau, :, :, :],
                                         rtl[:, :, :]))
                        if produce_xn:
                            self.bcast(m_ap, rs_ap)
                            self.wait(VE, self.ste)
                            for c in range(NCH):
                                em(VE, "scalar_tensor_tensor", rb[:, c, :],
                                   self.bank(6), -1.0, rtl[:, c, :],
                                   ALU.mult, ALU.add)
                            em(VE, "drain")
                            for c in range(NCH):
                                v = em(VE, "tensor_tensor",
                                       self.XN[:, c, tau * TAU:(tau + 1) * TAU],
                                       rb[:, c, :], self.bank(7), ALU.mult)
                            self.bcast_ve = self.sve.inc(v)
                    else:
                        otms = [tmpb.bitcast(F32).rearrange(
                                    "p a b -> p (a b)")[:, 0:D],
                                y1b.rearrange("p a c t -> p (a c t)"
                                              ).bitcast(F32)[:, 0:D]]
                        osem = [self.o16, self.o16b]
                        for st in range(NSUB):
                            bb = 6 if st % 2 == 0 else 4
                            par = st % 2
                            self.wait(TE, self.sve, rdone)
                            self.wait(TE, self.sse)
                            if bb == 6:
                                self.wait(TE, self.sve, self.bcast_ve)
                            for c in range(NCH):
                                mm = em(TE, "matmul",
                                        self.ps[:, 512 * bb + 128 * c:
                                                512 * bb + 128 * (c + 1)],
                                        rtl[:, c, st * 128:(st + 1) * 128],
                                        self.ident[:, :], start=True, stop=True,
                                        is_transpose=True)
                            self.ste.inc(mm)
                            self.wait(SE, self.ste)
                            self.wait(SE, osem[par], omk[par])
                            sI = em(SE, "copy", otms[par][:, :],
                                    self.ps[:, 512 * bb: 512 * bb + 1024])
                            self.sse.inc(sI)
                            self.wait(SY, self.sse)
                            omk[par] = osem[par].dinc(
                                em(SY, "dma_start",
                                   self.y_out[tau * TAU + st * 128:
                                              tau * TAU + (st + 1) * 128, :],
                                   otms[par][:, :]))
                if not final_out:
                    self.wait(SE, self.g16)
        self.barrier()

    def attn_b(self):
        nc, em = self.nc, self.em
        self.ctx = "attn_b"
        TE, SE, VE, GE, SY = self.TE, self.SE, self.VE, self.GE, self.SY
        with ExitStack() as ph:
            sb = lambda *a: ph.enter_context(nc.sbuf_tensor(*a))
            Bu = sb("b_bu", [128, 2, NKS, TAU], BF16)
            wsB = [sb("b_wsb%d" % i, [128, 2, NCH, SIG], BF16) for i in range(2)]
            wsC = [sb("b_wsc%d" % i, [128, 2, NKS, D], BF16) for i in range(2)]
            wsD = sb("b_wsd", [128, NCH, D], BF16)
            oacc = sb("b_oacc", [128, NH, NCH, TAU], F32)
            rtile = sb("b_rt", [128, NCH, TAU], F32)
            rb = sb("b_rb", [128, NCH, TAU], BF16)
            rsq = sb("b_rsq", [128, NCH, TAU], BF16)
            rrd = self.S("b_rrd", "SY")
            wload = [self.S("b_wl%d" % i, "SY") for i in range(2)]
            dload = self.S("b_dl", "SY")

            self.wait(SY, self.cstB, self.castB_done)

            def load_sigma(buf, sg):
                sem = wload[buf]
                for j, w in enumerate(["B_re", "B_im"]):
                    for kc in range(NCH):
                        sem.dinc(em(SY, "dma_start_transpose",
                                    wsB[buf][:, j, kc, :],
                                    self.scr[w][sg * SIG:(sg + 1) * SIG,
                                                kc * 128:(kc + 1) * 128]))
                for j, w in enumerate(["C_re", "C_im"]):
                    for kc in range(NKS):
                        sem.dinc(em(SY, "dma_start_transpose",
                                    wsC[buf][:, j, kc, :],
                                    self.scr[w][:, sg * SIG + kc * 128:
                                                sg * SIG + (kc + 1) * 128]))
                return sem.n

            nblk = 2 * NSIG
            marks = {}
            marks[0] = load_sigma(0, 0)
            self.load_wT(wsD, self.scr["Dm"], dload)
            marks[1] = load_sigma(1, 1)
            te_done = {}
            bu_guard = self.sse.n
            rrds_b = [rrd, self.S("b_rrd2", "SY")]
            xn_rt = [self.XN[:, :, h * HALF:(h + 1) * HALF].bitcast(F32)
                     for h in range(2)]
            tmk = {}

            def emit_tail_tau(h, ti):
                tau = h * NH + ti
                rtl = rtile if ti == 0 else xn_rt[h]
                self.wait(VE, rrds_b[ti], tmk[(h, ti)])
                em(VE, "drain")
                for c in range(NCH):
                    v = em(VE, "tensor_tensor", rtl[:, c, :],
                           oacc[:, ti, c, :], rtl[:, c, :], ALU.add)
                rdone = self.sve.inc(v)
                self.wait(SE, self.sve, rdone)
                self.wait(SE, self.ste)
                em(SE, "activation", rb[:, :, :], rtl[:, :, :], AF.Copy)
                sq = em(SE, "activation", rsq[:, :, :], rtl[:, :, :],
                        AF.Square)
                self.sse.inc(sq)
                self.wait(TE, self.sse)
                self.stats_mms(lambda c: rb[:, c, :], lambda c: rsq[:, c, :])
                self.stats_copy(tau)
                self.wait(GE, self.sve, rdone)
                self.g16.dinc(em(GE, "dma_start", self.scr_r[3][tau, :, :, :],
                                 rtl[:, :, :]))

            for blki in range(nblk):
                half, sg = blki // NSIG, blki % NSIG
                buf = blki % 2
                if sg == NSIG - 1:
                    self.wait(SY, self.sve)
                    self.wait(SY, self.g16)
                    tmk[(half, 0)] = rrds_b[0].dinc(
                        em(SY, "dma_start", rtile[:, :, :],
                           self.scr_r[2][half * NH, :, :, :]))
                self.wait(TE, wload[buf], marks[blki])
                for ti in range(NH):
                    tau = half * NH + ti
                    sse_base = self.sse.n
                    bm = {}
                    for idx in range(2 * NKS):
                        scl, j = idx // 2, idx % 2
                        sc = sg * NKS + scl
                        b = (4, 5, 6, 7)[idx % 4]
                        self.wait(TE, self.sse, bm.get(idx - 4, bu_guard))
                        for kc in range(NCH):
                            mm = em(TE, "matmul", self.bank(b),
                                    wsB[buf][:, j, kc, scl * 128:(scl + 1) * 128],
                                    self.XN[:, kc, tau * TAU:(tau + 1) * TAU],
                                    start=(kc == 0), stop=(kc == NCH - 1))
                        self.ste.inc(mm)
                        self.wait(SE, self.ste)
                        if idx == 0:
                            self.wait(SE, self.ste, te_done.get("out", 0))
                        sc_n = "g2" if j == 0 else "g2n"
                        bi_n = "cre2" if j == 0 else "cim2n"
                        s = em(SE, "activation", Bu[:, j, scl, :], self.bank(b),
                               AF.Identity, scale=self.cvc(sc_n, sc),
                               bias=self.cvc(bi_n, sc))
                        bm[idx] = self.sse.inc(s)
                    bu_guard = self.sse.n
                    if half == 1 and sg == 0:
                        emit_tail_tau(0, ti)
                    if half == 1 and sg == NSIG - 1 and ti == 1:
                        emit_tail_tau(1, 0)
                    if sg == 0:
                        self.wait(TE, dload, dload.n)
                    self.wait(TE, self.sse, bu_guard)
                    sve_base = self.sve.n
                    om = {}
                    obanks = (0, 1) if half == 1 else (0, 1, 2, 3)
                    for oc in range(NCH):
                        b = obanks[oc % len(obanks)]
                        self.wait(TE, self.sve,
                                  om.get(oc - len(obanks), sve_base))
                        nmm = 2 * NKS + (NCH if sg == 0 else 0)
                        i = 0
                        for j in range(2):
                            for scl in range(NKS):
                                mm = em(TE, "matmul", self.bank(b),
                                        wsC[buf][:, j, scl, oc * 128:(oc + 1) * 128],
                                        Bu[:, j, scl, :],
                                        start=(i == 0), stop=(i == nmm - 1))
                                i += 1
                        if sg == 0:
                            for kc in range(NCH):
                                mm = em(TE, "matmul", self.bank(b),
                                        wsD[:, kc, oc * 128:(oc + 1) * 128],
                                        self.XN[:, kc, tau * TAU:(tau + 1) * TAU],
                                        start=False, stop=(i == nmm - 1))
                                i += 1
                        self.ste.inc(mm)
                        self.wait(VE, self.ste)
                        if sg == 0:
                            v = em(VE, "tensor_copy", oacc[:, ti, oc, :],
                                   self.bank(b))
                        else:
                            v = em(VE, "tensor_tensor", oacc[:, ti, oc, :],
                                   oacc[:, ti, oc, :], self.bank(b), ALU.add)
                        om[oc] = self.sve.inc(v)
                te_done["out"] = self.ste.n
                if sg == NSIG - 1:
                    self.wait(SY, self.ste)
                    self.wait(SY, self.sve)
                    self.wait(SY, self.g16)
                    tmk[(half, 1)] = rrds_b[1].dinc(
                        em(SY, "dma_start", xn_rt[half][:, :, :],
                           self.scr_r[2][half * NH + 1, :, :, :]))
                if blki + 2 < nblk:
                    self.wait(SY, self.ste, te_done["out"])
                    marks[blki + 2] = load_sigma(buf, (blki + 2) % NSIG)
            emit_tail_tau(1, 1)
            self.fin_full()
        self.barrier()

    def emit(self):
        with self.nc.Block():
            self.setup()
            self.aw_scope = ExitStack()
            self.aws = [self.aw_scope.enter_context(
                self.nc.sbuf_tensor("aa_ws%d" % i, [128, NCH, D], BF16))
                for i in range(2)]
            self.stage0()
            self.wait(self.SY, self.cast16, self.cast_done)
            self.aw_marks = [
                self.load_wT(self.aws[0], self.scr["Wp2"], self.wsld[0]),
                self.load_wT(self.aws[1], self.scr["Wp1"], self.wsld[1])]
            self.attn_a()
            self.aw_scope.close()
            self.ffn(1, produce_xn=True, final_out=False)
            self.attn_b()
            self.ffn(3, produce_xn=False, final_out=True)
            self.wait(self.SY, self.o16)
            self.wait(self.SY, self.o16b)
            self.wait(self.GE, self.g16)
        self.es.close()


def build_nc():
    nc = bass.Bass("TRN2", target_bir_lowering=False)
    p = Prog(nc)
    p.emit()
    build_nc.te_log = p.te_log
    return nc


def kernel(**inputs):
    x = np.ascontiguousarray(inputs["L0_attn_a_x"])
    nc = build_nc()
    in_maps = []
    for i in range(8):
        m = {k: np.ascontiguousarray(v) for k, v in inputs.items()
             if k != "L0_attn_a_x"}
        m["L0_attn_a_x"] = np.ascontiguousarray(x[i])
        in_maps.append(m)
    res = run_bass_kernel_spmd(nc, in_maps, core_ids=list(range(8)))
    return np.stack([r["y"] for r in res.results], axis=0)
